# version 25
# baseline (speedup 1.0000x reference)
import sys
from contextlib import ExitStack
from dataclasses import dataclass
import numpy as np
import concourse.bass as bass
import concourse.mybir as mybir

F32 = mybir.dt.float32
BF16 = mybir.dt.bfloat16
F16 = mybir.dt.float16
AF = mybir.ActivationFunctionType
ALU = mybir.AluOpType
AX = mybir.AxisListType

NEG_BIG = -1.0e30
MAGIC = 1536.0


@dataclass(frozen=True)
class Cfg:
    T: int = 4096
    D: int = 1024
    HPC: int = 8
    I: int = 2816
    TQB: int = 512
    NB: int = 4
    DH: int = 64

    @property
    def HC(self): return self.D // 128
    @property
    def IC(self): return self.I // 128
    @property
    def NTK(self): return self.T // 128
    @property
    def NQB(self): return self.T // self.TQB
    @property
    def TOWN(self): return self.T // 2
    @property
    def NOB(self): return self.TOWN // self.TQB
    @property
    def PAIRS(self): return self.HPC // 2
    @property
    def DIAG(self): return self.TQB // 128


class Prog:
    ENGINES = ("sync", "gpsimd", "vector", "scalar", "tensor")

    def __init__(self):
        self.ops = {e: [] for e in self.ENGINES}
        self.cnt = {}
        self.last_wait = {e: {} for e in self.ENGINES}

    def val(self, sem):
        return self.cnt.get(sem, 0)

    def op(self, eng, fn, waits=(), inc=None, inc_amt=None):
        if inc_amt is None:
            inc_amt = 1 if inc in ("cc", "te", "dve", "act", "gps") else 16
        w = []
        for s, v in waits:
            if not v or v <= 0:
                continue
            if v > self.last_wait[eng].get(s, 0):
                self.last_wait[eng][s] = v
                w.append((s, v))
        if inc is not None:
            self.cnt[inc] = self.cnt.get(inc, 0) + inc_amt
        self.ops[eng].append((fn, w, inc, inc_amt))
        return self.cnt.get(inc)


DEBUG_SYNC = False


def build_program(cfg: Cfg):
    nc = bass.Bass("TRN2", target_bir_lowering=False)
    T, D, HPC, I, TQB = cfg.T, cfg.D, cfg.HPC, cfg.I, cfg.TQB
    HC, IC, NTK, NQB, TOWN, NOB, PAIRS, DIAG = (
        cfg.HC, cfg.IC, cfg.NTK, cfg.NQB, cfg.TOWN, cfg.NOB, cfg.PAIRS, cfg.DIAG)
    NT = T // 128
    NT_OWN = TOWN // 128
    G4 = 4 if NT % 4 == 0 else 2
    NGR = NT // G4
    NGR_OWN = NT_OWN // G4

    x_full = nc.dram_tensor("x_full", [T, D], F32, kind="ExternalInput").ap()
    x_own = nc.dram_tensor("x_own", [TOWN, D], F32, kind="ExternalInput").ap()
    wqkv = nc.dram_tensor("wqkv", [D, 3 * HPC * 64], F32, kind="ExternalInput").ap()
    bqkv = nc.dram_tensor("bqkv", [3 * HPC * 64], F32, kind="ExternalInput").ap()
    wproj = nc.dram_tensor("wproj", [HPC * 64, D], F32, kind="ExternalInput").ap()
    bproj = nc.dram_tensor("bproj", [D], F32, kind="ExternalInput").ap()
    clip_x = nc.dram_tensor("clip_x", [1], F32, kind="ExternalInput").ap()
    clip_down = nc.dram_tensor("clip_down", [1], F32, kind="ExternalInput").ap()
    wg_d = nc.dram_tensor("wg", [I, D], F32, kind="ExternalInput").ap()
    wu_d = nc.dram_tensor("wu", [I, D], F32, kind="ExternalInput").ap()
    wd_d = nc.dram_tensor("wd", [D, I], F32, kind="ExternalInput").ap()
    out_d = nc.dram_tensor("out", [D, TOWN], F32, kind="ExternalOutput").ap()
    cc1_in = nc.dram_tensor("cc1_in", [128], F32)
    cc1_out = nc.dram_tensor("cc1_out", [128], F32)
    cc2_in = nc.dram_tensor("cc2_in", [128], F32)
    cc2_out = nc.dram_tensor("cc2_out", [128], F32)
    po_d = nc.dram_tensor("po_d", [2, D, TOWN], BF16)
    attn_d = nc.dram_tensor("attn_d", [D, TOWN], BF16)

    P = Prog()
    st = ExitStack()
    _open = []

    def sbuf(name, shape, dt=F32):
        cm = nc.sbuf_tensor(name, shape, dt)
        t = cm.__enter__()
        _open.append((name, cm))
        return t

    def sbuf_free(*names):
        for name in names:
            nm, cm = _open.pop()
            assert nm == name, f"free order mismatch: {nm} != {name}"
            cm.__exit__(None, None, None)

    with st:
        ident_f32 = sbuf("ident_f32", [128, 128], F32)
        ident_bf = sbuf("ident_bf", [128, 128], BF16)
        masks_bf = sbuf("masks_bf", [128, DIAG * TQB], BF16)
        ones_row = sbuf("ones_row", [1, TQB], BF16)
        half_row = sbuf("half_row", [1, TQB], BF16)
        ones64f = sbuf("ones64f", [1, 64], F32)
        ones_p64 = sbuf("ones_p64", [65, 64], F32)
        ones128f = sbuf("ones128f", [1, 128], F32)
        slide = sbuf("slide", [128, NQB + 15], BF16)
        eps_b = sbuf("eps_b", [128, 1], F32)
        st1 = sbuf("st1", [128, 64], F32)
        xmax_acc = sbuf("xmax_acc", [128, 1], F32)
        hmax_acc = sbuf("hmax_acc", [128, 1], F32)
        gvec = sbuf("gvec", [1, 128], F32)
        scal = sbuf("scal", [1, 32], F32)
        sc_bc = sbuf("sc_bc", [128, 8], F32)
        n16 = sbuf("n16", [16, 1], F32)
        n16r = sbuf("n16r", [1, 16], F32)
        negM = sbuf("negM", [128, 2], F32)
        den128 = sbuf("den128", [128, max(2 * TQB // 128, 1)], F32)
        misc = sbuf("misc", [128, 16], F32)
        xn_fm = sbuf("xn_fm", [128, HC * T], BF16)
        wpair_b = sbuf("wpair_b", [128, HC * 384], BF16)
        waug_b = sbuf("waug_b", [128, HC * 130], BF16)
        bq_sb = sbuf("bq_sb", [1, 384], BF16)
        bvaug = sbuf("bvaug", [1, 130], BF16)
        k_pair = sbuf("k_pair", [128, T], BF16)
        q_pair = sbuf("q_pair", [128, T], BF16)
        v_aug = sbuf("v_aug", [128, NTK * 130], BF16)
        sqs = sbuf("sqs", [128, 2 * TQB], BF16)
        p_pair = sbuf("p_pair", [128, 2 * 2 * TQB], BF16)
        denrow = sbuf("denrow", [65, 2 * TQB], F32)
        wproj_b = sbuf("wproj_b", [64, HPC * D], BF16)
        bproj_b = sbuf("bproj_b", [1, D], BF16)
        po_st = sbuf("po_st", [128, 2 * TQB], BF16)
        denr = denrow[0:1, :]
        NXR = 2 * G4
        xt = sbuf("xt", [128, NXR * D], F32)
        sq_s = sbuf("sq_s", [128, D], F32)
        xnf = sbuf("xnf", [128, 2 * D], F32)

        ps_s0 = st.enter_context(nc.psum_tensor("ps_s0", [128, 2 * TQB], F32))
        ps_s1 = st.enter_context(nc.psum_tensor("ps_s1", [128, 2 * TQB], F32))
        ps_avA = st.enter_context(nc.psum_tensor("ps_avA", [128, TQB], F32))
        ps_avB = st.enter_context(nc.psum_tensor("ps_avB", [128, TQB], F32))
        ps_u0 = st.enter_context(nc.psum_tensor("ps_u0", [128, TQB], F32))
        ps_u1 = st.enter_context(nc.psum_tensor("ps_u1", [128, TQB], F32))

        SEM_NAMES = ("sclip", "sx0", "sx1", "swq0", "sbq", "swp0", "sbp",
                     "sgw0", "sgw1", "sgu0", "sgu1", "swd0", "swd1",
                     "sx5a", "sx5b", "sat0", "sat1",
                     "st_w0", "st_w1", "st_u0", "st_u1", "st_d0", "st_d1",
                     "gden", "gcc", "spo0", "spo1", "so0", "so1",
                     "cc", "te", "dve", "act", "gps")
        sem_h = {name: st.enter_context(nc.semaphore(name)) for name in SEM_NAMES}
        block = st.enter_context(nc.Block())

        def OP(eng, fn, waits=(), inc=None, amt=None):
            return P.op(eng, fn, waits=waits, inc=inc, inc_amt=amt)

        V = "vector"; S = "scalar"; TE = "tensor"; G = "gpsimd"; SY = "sync"

        def vdrain():
            OP(V, lambda v: v.drain())

        def sdrain():
            OP(S, lambda s: s.drain())

        OP(G, lambda g: g.memset(ident_f32[:], 0.0), inc="gps")
        OP(G, lambda g: g.memset(masks_bf[:], 0.0), inc="gps")
        OP(G, lambda g: g.memset(slide[:], 0.0), inc="gps")
        OP(G, lambda g: g.affine_select(out=ident_f32[:], in_=ident_f32[:],
                                        compare_op=ALU.not_equal, fill=1.0, base=0,
                                        pattern=[[-1, 128]], channel_multiplier=1),
           waits=[("gps", 3)], inc="gps")

        def mk_mask(dlt):
            def f(g):
                return g.affine_select(
                    out=masks_bf[:, dlt * TQB:(dlt + 1) * TQB],
                    in_=masks_bf[:, dlt * TQB:(dlt + 1) * TQB],
                    compare_op=ALU.is_ge, fill=NEG_BIG,
                    base=-128 * dlt, pattern=[[1, TQB]], channel_multiplier=-1)
            return f
        for dlt in range(DIAG):
            OP(G, mk_mask(dlt), inc="gps")
        OP(G, lambda g: g.affine_select(out=slide[:], in_=slide[:],
                                        compare_op=ALU.not_equal, fill=1.0,
                                        base=NQB - 1, pattern=[[-1, NQB + 15]],
                                        channel_multiplier=0),
           waits=[("gps", 4 + DIAG)], inc="gps")
        gps_const = P.val("gps")
        OP(V, lambda v: v.memset(ones_row[:], 1.0), inc="dve")
        OP(V, lambda v: v.memset(half_row[:], 0.5), inc="dve")
        OP(V, lambda v: v.memset(ones64f[:], 1.0), inc="dve")
        OP(V, lambda v: v.memset(ones_p64[64:65, :], 1.0), inc="dve")
        OP(V, lambda v: v.memset(ones128f[:], 1.0), inc="dve")
        OP(V, lambda v: v.memset(eps_b[:], 1e-5), inc="dve")
        OP(V, lambda v: v.memset(xmax_acc[:], 0.0), inc="dve")
        OP(V, lambda v: v.memset(hmax_acc[:], 0.0), inc="dve")
        OP(V, lambda v: v.tensor_copy(out=ident_bf[:], in_=ident_f32[:]),
           waits=[("gps", gps_const)], inc="dve")
        dve_const = P.val("dve")

        OP(SY, lambda s: s.dma_start(out=scal[0:1, 0:1], in_=clip_x[None, :]), inc="sclip")
        sd = OP(SY, lambda s: s.dma_start(out=scal[0:1, 1:2], in_=clip_down[None, :]), inc="sclip")
        act_sig = OP(S, lambda s: s.activation(out=scal[0:1, 2:4], in_=scal[0:1, 0:2],
                                               func=AF.Sigmoid), waits=[("sclip", sd)], inc="act")

        te_pair_prev = 0
        act_exp_par = {0: 0, 1: 0}
        act_last_exp = 0
        dve_y_evac_prev = 0
        v_evac = [0, 0]

        def emit_pair_weights(pr, extra_waits):
            wq_ld = 0
            for ccx in range(HC):
                wq_ld = OP(G, lambda g, ccx=ccx, pr=pr: g.dma_start(
                    out=wpair_b[:, ccx * 384:(ccx + 1) * 384]
                        .rearrange("p (s c) -> p s c", s=3),
                    in_=wqkv[ccx * 128:(ccx + 1) * 128, :]
                        .rearrange("p (s hc) -> p s hc", s=3)[:, :, pr * 128:(pr + 1) * 128]),
                    waits=extra_waits, inc="swq0")
            OP(V, lambda v: v.memset(waug_b[:], 0.0), waits=extra_waits, inc="dve")
            vdrain()
            for ccx in range(HC):
                OP(V, lambda v, ccx=ccx: v.tensor_copy(
                    out=waug_b[:, ccx * 130:(ccx + 1) * 130]
                        .rearrange("p (g c) -> p g c", g=2, c=65)[:, :, 0:64],
                    in_=wpair_b[:, ccx * 384 + 256: ccx * 384 + 384]
                        .rearrange("p (g c) -> p g c", g=2, c=64)),
                   waits=[("swq0", wq_ld)], inc="dve")
            bq_ld = OP(G, lambda g, pr=pr: g.dma_start(
                out=bq_sb[:].rearrange("one (s c) -> one s c", s=3),
                in_=bqkv[None, :].rearrange("one (s hc) -> one s hc", s=3)
                    [:, :, pr * 128:(pr + 1) * 128]),
                waits=extra_waits, inc="sbq")
            OP(V, lambda v: v.memset(bvaug[:], 1.0), waits=extra_waits, inc="dve")
            vdrain()
            OP(V, lambda v: v.tensor_copy(
                out=bvaug[:].rearrange("one (g c) -> one g c", g=2, c=65)[:, :, 0:64],
                in_=bq_sb[0:1, 256:384].rearrange("one (g c) -> one g c", g=2, c=64)),
               waits=[("sbq", bq_ld)], inc="dve")
            return {"wq_ld": wq_ld, "dve_wp": P.val("dve"),
                    "bankfree": [0, 0], "bseq": 0,
                    "sqs_te": [0, 0], "pending_norm": [],
                    "v_i": 0}

        def emit_kq_tb(stt, which, tb, banks, extra_waits=()):
            dst, sec, psoff = (q_pair, 0, 0) if which == "q" else (k_pair, 1, TQB)
            bk = stt["bseq"] % 2
            stt["bseq"] += 1
            ub = banks[bk]
            w = [("dve", stt["dve_wp"]), ("dve", stt["bankfree"][bk]),
                 ("swq0", stt["wq_ld"]), ("gps", gps_const)] + list(extra_waits)
            for ccx in range(HC):
                OP(TE, lambda t, ccx=ccx, sec=sec, tb=tb, ub=ub: t.matmul(
                    ub[:, 0:TQB],
                    wpair_b[:, ccx * 384 + sec * 128: ccx * 384 + sec * 128 + 128],
                    xn_fm[:, ccx * T + tb * TQB: ccx * T + (tb + 1) * TQB],
                    start=(ccx == 0), stop=False), waits=w, inc="te")
                w = []
            tb_mm = OP(TE, lambda t, sec=sec, ub=ub: t.matmul(
                ub[:, 0:TQB], bq_sb[0:1, sec * 128:sec * 128 + 128],
                ones_row[:], start=False, stop=True), inc="te")
            if stt["pending_norm"]:
                stt["pending_norm"].pop(0)()
            ev = OP(V, lambda v, dst=dst, tb=tb, ub=ub: v.tensor_copy(
                out=dst[:, tb * TQB:(tb + 1) * TQB], in_=ub[:, 0:TQB]),
                waits=[("te", tb_mm)], inc="dve")
            vdrain()
            OP(V, lambda v, dst=dst, tb=tb: v.scalar_tensor_tensor(
                out=sqs[:, (tb % 2) * TQB:(tb % 2 + 1) * TQB],
                in0=dst[:, tb * TQB:(tb + 1) * TQB], scalar=1.0,
                in1=dst[:, tb * TQB:(tb + 1) * TQB],
                op0=ALU.mult, op1=ALU.mult),
               waits=[("te", stt["sqs_te"][tb % 2])], inc="dve")
            stt["bankfree"][bk] = P.val("dve")
            dvn = P.val("dve")

            def norm_emit(tb=tb, psoff=psoff, dvn=dvn):
                def doit():
                    r_ = OP(TE, lambda t: t.matmul(
                        ps_s1[0:16, psoff:psoff + TQB],
                        slide[:, NQB - 1 - tb: NQB - 1 - tb + 16],
                        sqs[:, (tb % 2) * TQB:(tb % 2 + 1) * TQB],
                        start=(tb == 0), stop=(tb == NQB - 1)),
                       waits=[("dve", dvn), ("act", act_last_exp)], inc="te")
                    stt["sqs_te"][tb % 2] = r_
                return doit
            stt["pending_norm"].append(norm_emit())

        def emit_v_batch(stt, banks, nbmax, extra_waits=()):
            vt = stt["v_i"]
            nb3 = min(nbmax, NTK - vt, max(1, TQB // 130))
            if nb3 <= 0:
                return
            bk = stt["bseq"] % 2
            stt["bseq"] += 1
            ub = banks[bk]
            w = [("dve", stt["dve_wp"]), ("dve", stt["bankfree"][bk])] + list(extra_waits)
            for b3 in range(nb3):
                for ccx in range(HC):
                    OP(TE, lambda t, vt=vt, b3=b3, ccx=ccx, ub=ub: t.matmul(
                        ub[:, b3 * 130:(b3 + 1) * 130],
                        xn_fm[:, ccx * T + (vt + b3) * 128: ccx * T + (vt + b3 + 1) * 128],
                        waug_b[:, ccx * 130:(ccx + 1) * 130],
                        start=(ccx == 0), stop=False), waits=w, inc="te")
                    w = []
                OP(TE, lambda t, b3=b3, ub=ub: t.matmul(
                    ub[:, b3 * 130:(b3 + 1) * 130], ones_row[0:1, 0:128],
                    bvaug[:], start=False, stop=True), inc="te")
            tv = P.val("te")
            ev_ = OP(V, lambda v, vt=vt, nb3=nb3, ub=ub: v.tensor_copy(
                out=v_aug[:, vt * 130:(vt + nb3) * 130], in_=ub[:, 0:nb3 * 130]),
                waits=[("te", tv)], inc="dve")
            stt["bankfree"][bk] = ev_
            v_evac[bk] = ev_
            stt["v_i"] = vt + nb3

        def emit_norm_tail(stt, parP):
            while stt["pending_norm"]:
                stt["pending_norm"].pop(0)()
            te_norm = P.val("te")
            OP(V, lambda v: v.tensor_reduce(out=n16[:, 0:1], in_=ps_s1[0:16, 0:TQB],
                                            axis=AX.X, op=ALU.max),
               waits=[("te", te_norm)], inc="dve")
            OP(V, lambda v: v.tensor_reduce(out=misc[0:16, 0:1],
                                            in_=ps_s1[0:16, TQB:2 * TQB],
                                            axis=AX.X, op=ALU.max), inc="dve")
            vdrain()
            OP(V, lambda v: v.tensor_tensor(out=n16[:, 0:1], in0=n16[:, 0:1],
                                            in1=misc[0:16, 0:1], op=ALU.max), inc="dve")
            dvn16 = P.val("dve")
            gn = OP(G, lambda g: g.dma_start(out=n16r[:], in_=n16[:, 0:1]),
                    waits=[("dve", dvn16)], inc="gden")
            OP(V, lambda v: v.tensor_reduce(out=scal[0:1, 10:11], in_=n16r[:],
                                            axis=AX.X, op=ALU.max),
               waits=[("gden", gn)], inc="dve")
            vdrain()
            OP(V, lambda v: v.tensor_scalar(out=scal[0:1, 11:12], in0=scal[0:1, 10:11],
                                            scalar1=-0.125, scalar2=None, op0=ALU.mult),
               inc="dve")
            dvm = P.val("dve")
            tm = OP(TE, lambda t: t.matmul(ps_u1[:, 0:1], ones128f[:], scal[0:1, 11:12],
                                           start=True, stop=True),
                    waits=[("dve", dvm), ("dve", stt["bankfree"][1])], inc="te")
            OP(V, lambda v, parP=parP: v.tensor_copy(out=negM[:, parP:parP + 1],
                                                     in_=ps_u1[:, 0:1]),
               waits=[("te", tm), ("act", act_exp_par[0]), ("act", act_exp_par[1])],
               inc="dve")
            return P.val("dve")

        pair_state = emit_pair_weights(0, [("gps", gps_const)])

        def p1_hook(g_i):
            if g_i >= NQB:
                return
            ew = [("dve", dve_ev_done[min(g_i * G4 + G4 - 1, NT - 1)])]
            emit_kq_tb(pair_state, "q", g_i, (ps_avA, ps_avB), ew)
            emit_kq_tb(pair_state, "k", g_i, (ps_avA, ps_avB), ew)
            while pair_state["v_i"] < min((g_i + 1) * G4, NTK):
                emit_v_batch(pair_state, (ps_avA, ps_avB), 2, ew)

        P._p1_hook = p1_hook

        dve_tile_done = [0] * NT
        te_tr_done = [0] * NT
        dve_ev_done = [0] * NT
        act_grp_done = [0] * NGR
        dve_grp_done = [0] * NGR
        for g_i in range(NGR):
            gp = g_i % 2
            sb32 = gp * 32
            sxn = "sx0" if gp == 0 else "sx1"
            sds = [0] * G4
            for k in range(G4):
                i = g_i * G4 + k
                slot = (i % NXR)
                xsl = xt[:, slot * D:(slot + 1) * D]
                w = []
                if g_i >= 2:
                    w = [("dve", dve_grp_done[g_i - 2]), ("act", act_grp_done[g_i - 2])]
                sds[k] = OP(SY, lambda s, i=i, xsl=xsl: s.dma_start(
                    out=xsl, in_=x_full[i * 128:(i + 1) * 128, :]), waits=w, inc=sxn)
            for k in range(G4):
                i = g_i * G4 + k
                slot = (i % NXR)
                xsl = xt[:, slot * D:(slot + 1) * D]
                sqsl = sq_s[:, 0:D]
                OP(V, lambda v, xsl=xsl, c=sb32 + k: v.tensor_reduce(
                    out=st1[:, c:c + 1], in_=xsl, axis=AX.X, op=ALU.add),
                   waits=[(sxn, sds[G4 - 1])], inc="dve")
                sdrain()
                OP(S, lambda s, xsl=xsl, sqsl=sqsl, c=sb32 + 4 + k: s.activation(
                    out=sqsl, in_=xsl, func=AF.Square, accum_out=st1[:, c:c + 1]),
                   waits=[(sxn, sds[G4 - 1]),
                          ("dve", dve_grp_done[g_i - 2] if g_i >= 2 else 0)], inc="act")
            a_sq = P.val("act")
            vdrain()
            OP(V, lambda v, b=sb32: v.tensor_scalar(
                out=st1[:, b + 8:b + 8 + G4], in0=st1[:, b:b + G4],
                scalar1=1.0 / D, scalar2=None, op0=ALU.mult),
               waits=[("act", a_sq)], inc="dve")
            vdrain()
            OP(V, lambda v, b=sb32: v.tensor_tensor(
                out=st1[:, b + 12:b + 12 + G4], in0=st1[:, b + 8:b + 8 + G4],
                in1=st1[:, b + 8:b + 8 + G4], op=ALU.mult), inc="dve")
            vdrain()
            OP(V, lambda v, b=sb32: v.scalar_tensor_tensor(
                out=st1[:, b + 16:b + 16 + G4], in0=st1[:, b + 4:b + 4 + G4],
                scalar=1.0 / D, in1=st1[:, b + 12:b + 12 + G4],
                op0=ALU.mult, op1=ALU.subtract), inc="dve")
            dv = P.val("dve")
            sdrain()
            a_rt = OP(S, lambda s, b=sb32: s.activation(
                out=st1[:, b + 20:b + 20 + G4], in_=st1[:, b + 16:b + 16 + G4],
                func=AF.Sqrt, bias=eps_b[:]), waits=[("dve", dv)], inc="act")
            OP(V, lambda v, b=sb32: v.reciprocal(
                out=st1[:, b + 20:b + 20 + G4], in_=st1[:, b + 20:b + 20 + G4]),
               waits=[("act", a_rt)], inc="dve")
            vdrain()
            OP(V, lambda v, b=sb32: v.scalar_tensor_tensor(
                out=st1[:, b + 24:b + 24 + G4], in0=st1[:, b + 8:b + 8 + G4],
                scalar=-1.0, in1=st1[:, b + 20:b + 20 + G4],
                op0=ALU.mult, op1=ALU.mult), inc="dve")
            vdrain()
            for k in range(G4):
                i = g_i * G4 + k
                slot = (i % NXR)
                r2 = i % 2
                xsl = xt[:, slot * D:(slot + 1) * D]
                xnfsl = xnf[:, r2 * D:(r2 + 1) * D]
                OP(V, lambda v, xsl=xsl, xnfsl=xnfsl, b=sb32, k=k: v.tensor_scalar(
                    out=xnfsl, in0=xsl, scalar1=st1[:, b + 20 + k:b + 21 + k],
                    scalar2=st1[:, b + 24 + k:b + 25 + k], op0=ALU.mult, op1=ALU.add),
                   waits=[("te", te_tr_done[i - 2] if i >= 2 else 0)], inc="dve")
                vdrain()
                OP(V, lambda v, xnfsl=xnfsl, b=sb32, k=k: v.tensor_reduce(
                    out=st1[:, b + 28 + k:b + 29 + k], in_=xnfsl, axis=AX.X,
                    op=ALU.max, apply_absolute_value=True), inc="dve")
                dvx = P.val("dve")
                for fc in range(HC):
                    if i % 2 == 0:
                        ub = ps_u0 if fc < HC // 2 else ps_u1
                    else:
                        ub = ps_s0[:, 0:TQB] if fc < HC // 2 else ps_s0[:, TQB:2 * TQB]
                    fcl = fc % (HC // 2)
                    OP(TE, lambda t, fc=fc, fcl=fcl, ub=ub, xnfsl=xnfsl: t.matmul(
                        ub[:, fcl * 128:(fcl + 1) * 128],
                        xnfsl[:, fc * 128:(fc + 1) * 128], ident_f32[:],
                        is_transpose=True, start=True, stop=True),
                       waits=[("dve", dvx if fc == 0 else 0),
                              ("dve", dve_ev_done[i - 2] if (fc == 0 and i >= 2) else 0)],
                       inc="te")
                te_tr_done[i] = P.val("te")
                for half in range(2):
                    if i % 2 == 0:
                        ub = ps_u0 if half == 0 else ps_u1
                    else:
                        ub = ps_s0[:, 0:TQB] if half == 0 else ps_s0[:, TQB:2 * TQB]
                    nch = HC // 2
                    OP(V, lambda v, i=i, half=half, ub=ub, nch=nch: v.tensor_copy(
                        out=xn_fm[:].rearrange("p (c t) -> p c t", c=HC)
                            [:, half * nch:(half + 1) * nch, i * 128:(i + 1) * 128],
                        in_=ub[:, 0:nch * 128].rearrange("p (c t) -> p c t", c=nch)),
                       waits=[("te", te_tr_done[i])], inc="dve")
                dve_ev_done[i] = P.val("dve")
                dve_tile_done[i] = P.val("dve")
            vdrain()
            OP(V, lambda v, b=sb32: v.tensor_reduce(
                out=st1[:, b + 28:b + 29], in_=st1[:, b + 28:b + 28 + G4],
                axis=AX.X, op=ALU.max), inc="dve")
            vdrain()
            OP(V, lambda v, b=sb32: v.tensor_tensor(
                out=xmax_acc[:], in0=xmax_acc[:], in1=st1[:, b + 28:b + 29],
                op=ALU.max), inc="dve")
            dve_grp_done[g_i] = P.val("dve")
            act_grp_done[g_i] = P.val("act")
            if getattr(P, "_p1_hook", None) is not None:
                P._p1_hook(g_i)
        dve_p1 = P.val("dve")

        g1 = OP(G, lambda g: g.dma_start(out=cc1_in[:], in_=xmax_acc[:, 0]),
                waits=[("dve", dve_p1), ("gps", gps_const)], inc="gcc")
        c1 = OP(G, lambda g: g.collective_compute(
            "AllReduce", ALU.max, replica_groups=[list(range(8))],
            ins=[cc1_in.ap()], outs=[cc1_out.ap()]), waits=[("gcc", g1)], inc="cc")
        g2 = OP(G, lambda g: g.dma_start(out=gvec[:], in_=cc1_out.ap()[None, :]),
                waits=[("cc", c1)], inc="gcc")

        sbuf_free("xnf", "sq_s", "xt")
        y2 = sbuf("y2", [64, HPC * T], BF16)
        p3_gate = []
        wp_ld = 0
        for hh in range(HPC):
            wp_ld = OP(G, lambda g, hh=hh: g.dma_start(
                out=wproj_b[:, hh * D:(hh + 1) * D],
                in_=wproj[hh * 64:(hh + 1) * 64, :]), waits=p3_gate, inc="swp0")
        bp_ld = OP(G, lambda g: g.dma_start(out=bproj_b[:], in_=bproj[None, :]),
                   waits=p3_gate, inc="sbp")
        dve_wproj = P.val("dve")


        pend_tail = [None]
        nrm_done = {}
        po_store = [0, 0]
        po_evac = [0, 0]
        po_ti = [0]

        def emit_po_tb(tb, dve_gate):
            for co in range(HC):
                r = po_ti[0] % 2
                po_ti[0] += 1
                ub = ps_u0 if r == 0 else ps_u1
                w = [("dve", dve_gate), ("swp0", wp_ld), ("sbp", bp_ld),
                     ("dve", po_evac[r]), ("dve", v_evac[1])]
                for hh in range(HPC):
                    OP(TE, lambda t, hh=hh, co=co, tb=tb, ub=ub: t.matmul(
                        ub[:, 0:TQB],
                        wproj_b[:, hh * D + co * 128: hh * D + (co + 1) * 128],
                        y2[0:64, hh * T + tb * TQB: hh * T + (tb + 1) * TQB],
                        start=(hh == 0), stop=False), waits=w, inc="te")
                    w = []
                OP(TE, lambda t, co=co, ub=ub: t.matmul(
                    ub[:, 0:TQB], bproj_b[0:1, co * 128:(co + 1) * 128],
                    half_row[:], start=False, stop=True), inc="te")
                tpo = P.val("te")
                spn = "spo0" if r == 0 else "spo1"
                ev = OP(V, lambda v, r=r, ub=ub: v.tensor_copy(
                    out=po_st[:, r * TQB:(r + 1) * TQB], in_=ub[:, 0:TQB]),
                    waits=[("te", tpo), (spn, po_store[r])], inc="dve")
                po_evac[r] = ev
                half, tbw = tb // NOB, tb % NOB
                po_store[r] = OP(G, lambda g, half=half, co=co, tbw=tbw, r=r: g.dma_start(
                    out=po_d.ap()[half, co * 128:(co + 1) * 128, tbw * TQB:(tbw + 1) * TQB],
                    in_=po_st[:, r * TQB:(r + 1) * TQB]), waits=[("dve", ev)], inc=spn)

        for pr in range(PAIRS):
            parP = pr % 2
            if pr > 0:
                pair_state = emit_pair_weights(pr, [("te", te_pair_prev)])
                for tb in range(NQB):
                    emit_kq_tb(pair_state, "q", tb, (ps_u0, ps_u1))
                for tb in range(NQB):
                    emit_kq_tb(pair_state, "k", tb, (ps_u0, ps_u1))
                while pair_state["v_i"] < NTK:
                    emit_v_batch(pair_state, (ps_u0, ps_u1), 3)
            dve_negM = emit_norm_tail(pair_state, parP)
            dve_v = P.val("dve")

            hA, hB = 2 * pr, 2 * pr + 1
            for qb in range(NQB):
                span = (qb + 1) * DIAG
                exp_done = {}
                tsc = {}

                def emit_scores(j):
                    par = j % 2
                    ps_sc = ps_s0 if par == 0 else ps_s1
                    w = [("dve", dve_v), ("dve", dve_negM), ("act", act_exp_par[par])]
                    is_diag = j >= qb * DIAG
                    OP(TE, lambda t, j=j, qb=qb, ps_sc=ps_sc, is_diag=is_diag: t.matmul(
                        ps_sc[:, 0:TQB], k_pair[0:64, j * 128:(j + 1) * 128],
                        q_pair[0:64, qb * TQB:(qb + 1) * TQB],
                        start=True, stop=not is_diag, tile_position=(0, 0)),
                       waits=w, inc="te")
                    OP(TE, lambda t, j=j, qb=qb, ps_sc=ps_sc, is_diag=is_diag: t.matmul(
                        ps_sc[:, TQB:2 * TQB], k_pair[64:128, j * 128:(j + 1) * 128],
                        q_pair[64:128, qb * TQB:(qb + 1) * TQB],
                        start=True, stop=not is_diag, tile_position=(64, 0)), inc="te")
                    if is_diag:
                        dlt = j - qb * DIAG
                        OP(TE, lambda t, dlt=dlt, ps_sc=ps_sc: t.matmul(
                            ps_sc[:, 0:TQB], ident_bf[:],
                            masks_bf[:, dlt * TQB:(dlt + 1) * TQB],
                            start=False, stop=True), inc="te")
                        OP(TE, lambda t, dlt=dlt, ps_sc=ps_sc: t.matmul(
                            ps_sc[:, TQB:2 * TQB], ident_bf[:],
                            masks_bf[:, dlt * TQB:(dlt + 1) * TQB],
                            start=False, stop=True), inc="te")
                    tsc[j] = P.val("te")

                def emit_exp(j):
                    par = j % 2
                    ps_sc = ps_s0 if par == 0 else ps_s1
                    exp_done[j] = OP(S, lambda s, par=par, ps_sc=ps_sc, parP=parP: s.activation(
                        out=p_pair[:, par * 2 * TQB:(par + 1) * 2 * TQB],
                        in_=ps_sc[:, 0:2 * TQB], func=AF.Exp, scale=0.125,
                        bias=negM[:, parP:parP + 1]),
                        waits=[("te", tsc[j])], inc="act")
                    act_exp_par[par] = exp_done[j]

                def emit_av(j):
                    par = j % 2
                    avw = [("act", exp_done[j])]
                    if j == 0:
                        avw.append(("dve", dve_y_evac_prev))
                    OP(TE, lambda t, j=j, par=par, first=(j == 0), last=(j == span - 1): t.matmul(
                        ps_avA[0:65, :], v_aug[:, j * 130: j * 130 + 65],
                        p_pair[:, par * 2 * TQB: par * 2 * TQB + TQB],
                        start=first, stop=last), waits=avw, inc="te")
                    OP(TE, lambda t, j=j, par=par, first=(j == 0), last=(j == span - 1): t.matmul(
                        ps_avB[0:65, :], v_aug[:, j * 130 + 65: j * 130 + 130],
                        p_pair[:, par * 2 * TQB + TQB: (par + 1) * 2 * TQB],
                        start=first, stop=last), inc="te")

                emit_scores(0)
                emit_exp(0)
                for j in range(1, span):
                    emit_scores(j)
                    emit_exp(j)
                    if j == 2 and pend_tail[0] is not None:
                        if pr == PAIRS - 1 and qb >= 2:
                            emit_po_tb(qb - 2, nrm_done[qb - 2])
                        pend_tail[0]()
                        pend_tail[0] = None
                    emit_av(j - 1)
                if pend_tail[0] is not None:
                    pend_tail[0]()
                    pend_tail[0] = None
                emit_av(span - 1)
                act_last_exp = exp_done[span - 1]
                tav = P.val("te")
                OP(V, lambda v, hA=hA, qb=qb: v.tensor_copy(
                    out=y2[:, hA * T + qb * TQB: hA * T + (qb + 1) * TQB],
                    in_=ps_avA[0:64, :]), waits=[("te", tav)], inc="dve")
                OP(V, lambda v, hB=hB, qb=qb: v.tensor_copy(
                    out=y2[:, hB * T + qb * TQB: hB * T + (qb + 1) * TQB],
                    in_=ps_avB[0:64, :]), inc="dve")
                OP(V, lambda v: v.tensor_copy(out=denrow[64:65, 0:TQB],
                                              in_=ps_avA[64:65, :]), inc="dve")
                OP(V, lambda v: v.tensor_copy(out=denrow[64:65, TQB:2 * TQB],
                                              in_=ps_avB[64:65, :]), inc="dve")
                dv_den = P.val("dve")
                dve_y_evac_prev = dv_den

                def late_tail(dv_den=dv_den, hA=hA, hB=hB, qb=qb):
                    vdrain()
                    OP(V, lambda v: v.reciprocal(
                        out=denrow[64:65, 0:2 * TQB], in_=denrow[64:65, 0:2 * TQB]),
                       waits=[("dve", dv_den)], inc="dve")
                    vdrain()
                    dvr = P.val("dve")
                    for hh, off in ((hA, 0), (hB, TQB)):
                        tb_ = OP(TE, lambda t, off=off: t.matmul(
                            ps_u1[0:64, 0:TQB], ones_p64[64:65, 0:64],
                            denrow[64:65, off:off + TQB],
                            start=True, stop=True, tile_position=(64, 0)),
                            waits=[("dve", dvr), ("dve", v_evac[1]),
                                   ("dve", po_evac[1])], inc="te")
                        nrm = OP(V, lambda v, hh=hh, qb=qb: v.tensor_tensor(
                            out=y2[:, hh * T + qb * TQB: hh * T + (qb + 1) * TQB],
                            in0=y2[:, hh * T + qb * TQB: hh * T + (qb + 1) * TQB],
                            in1=ps_u1[0:64, 0:TQB], op=ALU.mult),
                           waits=[("te", tb_)], inc="dve")
                        v_evac[1] = nrm
                        vdrain()
                    nrm_done[qb] = P.val("dve")
                pend_tail[0] = late_tail
            if pend_tail[0] is not None:
                if pr == PAIRS - 1:
                    emit_po_tb(NQB - 2, nrm_done[NQB - 2])
                pend_tail[0]()
                pend_tail[0] = None
                if pr == PAIRS - 1:
                    emit_po_tb(NQB - 1, P.val("dve"))
            te_pair_prev = P.val("te")
        dve_y = P.val("dve")

        c3 = OP(G, lambda g: g.collective_compute(
            "ReduceScatter", ALU.add,
            replica_groups=[[0, 1], [2, 3], [4, 5], [6, 7]],
            ins=[po_d.ap()], outs=[attn_d.ap()]),
            waits=[("spo0", P.val("spo0")), ("spo1", P.val("spo1"))], inc="cc")

        sbuf_free("y2")
        sbuf_free("po_st", "bproj_b", "wproj_b", "denrow", "p_pair", "sqs",
                  "v_aug", "q_pair", "k_pair", "bvaug", "bq_sb", "waug_b",
                  "wpair_b")
        sbuf_free("xn_fm")
        h_t = sbuf("h_t", [128, IC * TOWN], F16)
        xq_fm = sbuf("xq_fm", [128, HC * TOWN], BF16)
        xt4 = sbuf("xt4", [128, NXR * D], F32)
        sq4 = sbuf("sq4", [128, D], F32)
        xqst = sbuf("xqst", [128, 2 * D], F32)
        q16 = sbuf("q16", [128, 2 * D], F16)
        xq_tm = sbuf("xq_tm", [128, 2 * D], F32)
        OP(V, lambda v: v.tensor_reduce(out=scal[0:1, 4:5], in_=gvec[:], axis=AX.X,
                                        op=ALU.max), waits=[("gcc", g2)], inc="dve")
        vdrain()
        OP(V, lambda v: v.scalar_tensor_tensor(out=scal[0:1, 5:6], in0=scal[0:1, 4:5],
                                               scalar=1.0 / 127.0, in1=scal[0:1, 2:3],
                                               op0=ALU.mult, op1=ALU.mult),
           waits=[("act", act_sig)], inc="dve")
        vdrain()
        OP(V, lambda v: v.reciprocal(out=scal[0:1, 6:7], in_=scal[0:1, 5:6]), inc="dve")
        vdrain()
        OP(V, lambda v: v.tensor_scalar(out=scal[0:1, 7:8], in0=scal[0:1, 5:6],
                                        scalar1=1.0 / 127.0, scalar2=None, op0=ALU.mult),
           inc="dve")
        dve_sx0 = P.val("dve")
        t_bc = OP(TE, lambda t: t.matmul(ps_u1[:, 0:2], ones128f[:],
                                         scal[0:1, 6:8], start=True, stop=True),
                  waits=[("dve", dve_sx0)], inc="te")
        OP(V, lambda v: v.tensor_copy(out=sc_bc[:, 0:2], in_=ps_u1[:, 0:2]),
           waits=[("te", t_bc)], inc="dve")
        dve_scbc = P.val("dve")
        p4_gate = [("dve", P.val("dve")), ("te", P.val("te")),
                   ("act", P.val("act")), ("spo0", P.val("spo0")),
                   ("spo1", P.val("spo1")), ("gden", P.val("gden")),
                   ("gcc", P.val("gcc"))]
        te_xtr = [0] * NT_OWN
        dve_xev = [0] * NT_OWN
        dve_g_done = [0] * NGR_OWN
        act_g_done = [0] * NGR_OWN
        for g_i in range(NGR_OWN):
            gp = g_i % 2
            sb32 = gp * 32
            sxn = "sx0" if gp == 0 else "sx1"
            sds = [0] * G4
            for k in range(G4):
                i = g_i * G4 + k
                slot = i % NXR
                xsl = xt4[:, slot * D:(slot + 1) * D]
                w = list(p4_gate) + [("dve", dve_scbc)]
                if g_i >= 2:
                    w = [("dve", dve_g_done[g_i - 2]), ("act", act_g_done[g_i - 2])]
                sds[k] = OP(SY, lambda s, i=i, xsl=xsl: s.dma_start(
                    out=xsl, in_=x_own[i * 128:(i + 1) * 128, :]), waits=w, inc=sxn)
            for k in range(G4):
                i = g_i * G4 + k
                slot = i % NXR
                xsl = xt4[:, slot * D:(slot + 1) * D]
                sqsl = sq4[:, 0:D]
                OP(V, lambda v, xsl=xsl, c=sb32 + k: v.tensor_reduce(
                    out=st1[:, c:c + 1], in_=xsl, axis=AX.X, op=ALU.add),
                   waits=[(sxn, sds[G4 - 1])], inc="dve")
                sdrain()
                OP(S, lambda s, xsl=xsl, sqsl=sqsl, c=sb32 + 4 + k: s.activation(
                    out=sqsl, in_=xsl, func=AF.Square, accum_out=st1[:, c:c + 1]),
                   waits=[(sxn, sds[G4 - 1]),
                          ("dve", dve_g_done[g_i - 2] if g_i >= 2 else 0)], inc="act")
            a_sq = P.val("act")
            vdrain()
            OP(V, lambda v, b=sb32: v.tensor_scalar(
                out=st1[:, b + 8:b + 8 + G4], in0=st1[:, b:b + G4],
                scalar1=1.0 / D, scalar2=None, op0=ALU.mult),
               waits=[("act", a_sq)], inc="dve")
            vdrain()
            OP(V, lambda v, b=sb32: v.tensor_tensor(
                out=st1[:, b + 12:b + 12 + G4], in0=st1[:, b + 8:b + 8 + G4],
                in1=st1[:, b + 8:b + 8 + G4], op=ALU.mult), inc="dve")
            vdrain()
            OP(V, lambda v, b=sb32: v.scalar_tensor_tensor(
                out=st1[:, b + 16:b + 16 + G4], in0=st1[:, b + 4:b + 4 + G4],
                scalar=1.0 / D, in1=st1[:, b + 12:b + 12 + G4],
                op0=ALU.mult, op1=ALU.subtract), inc="dve")
            dv = P.val("dve")
            sdrain()
            a_rt = OP(S, lambda s, b=sb32: s.activation(
                out=st1[:, b + 20:b + 20 + G4], in_=st1[:, b + 16:b + 16 + G4],
                func=AF.Sqrt, bias=eps_b[:]), waits=[("dve", dv)], inc="act")
            OP(V, lambda v, b=sb32: v.reciprocal(
                out=st1[:, b + 20:b + 20 + G4], in_=st1[:, b + 20:b + 20 + G4]),
               waits=[("act", a_rt)], inc="dve")
            vdrain()
            OP(V, lambda v, b=sb32: v.scalar_tensor_tensor(
                out=st1[:, b + 24:b + 24 + G4], in0=st1[:, b + 8:b + 8 + G4],
                scalar=-1.0, in1=st1[:, b + 20:b + 20 + G4],
                op0=ALU.mult, op1=ALU.mult), inc="dve")
            vdrain()
            OP(V, lambda v, b=sb32: v.tensor_scalar(
                out=st1[:, b + 20:b + 20 + G4], in0=st1[:, b + 20:b + 20 + G4],
                scalar1=sc_bc[:, 0:1], scalar2=None, op0=ALU.mult), inc="dve")
            OP(V, lambda v, b=sb32: v.tensor_scalar(
                out=st1[:, b + 24:b + 24 + G4], in0=st1[:, b + 24:b + 24 + G4],
                scalar1=sc_bc[:, 0:1], scalar2=None, op0=ALU.mult), inc="dve")
            vdrain()
            for k in range(G4):
                i = g_i * G4 + k
                slot = i % NXR
                r2 = i % 2
                xsl = xt4[:, slot * D:(slot + 1) * D]
                sqsl = xqst[:, r2 * D:(r2 + 1) * D]
                f16sl = q16[:, r2 * D:(r2 + 1) * D]
                qsl = xq_tm[:, r2 * D:(r2 + 1) * D]
                OP(V, lambda v, xsl=xsl, sqsl=sqsl, b=sb32, k=k: v.tensor_scalar(
                    out=sqsl, in0=xsl, scalar1=st1[:, b + 20 + k:b + 21 + k],
                    scalar2=st1[:, b + 24 + k:b + 25 + k], op0=ALU.mult, op1=ALU.add),
                   waits=[("act", act_g_done[g_i - 2] if g_i >= 2 else 0),
                          ("act", act_last_exp),
                          ("te", te_xtr[i - 2] if i >= 2 else 0)], inc="dve")
                vdrain()
                OP(V, lambda v, sqsl=sqsl: v.tensor_scalar(
                    out=sqsl, in0=sqsl, scalar1=127.0, scalar2=-128.0,
                    op0=ALU.min, op1=ALU.max), inc="dve")
                vdrain()
                OP(V, lambda v, sqsl=sqsl, f16sl=f16sl: v.tensor_scalar(
                    out=f16sl, in0=sqsl, scalar1=MAGIC, scalar2=None, op0=ALU.add),
                   inc="dve")
                vdrain()
                OP(V, lambda v, f16sl=f16sl, qsl=qsl: v.tensor_scalar(
                    out=qsl, in0=f16sl, scalar1=-MAGIC, scalar2=None, op0=ALU.add),
                   inc="dve")
                dvx = P.val("dve")
                for fc in range(HC):
                    if i % 2 == 0:
                        ub = ps_u0 if fc < HC // 2 else ps_u1
                    else:
                        ub = ps_s0[:, 0:TQB] if fc < HC // 2 else ps_s0[:, TQB:2 * TQB]
                    fcl = fc % (HC // 2)
                    OP(TE, lambda t, fc=fc, fcl=fcl, ub=ub, qsl=qsl: t.matmul(
                        ub[:, fcl * 128:(fcl + 1) * 128],
                        qsl[:, fc * 128:(fc + 1) * 128], ident_f32[:],
                        is_transpose=True, start=True, stop=True),
                       waits=[("dve", dvx if fc == 0 else 0),
                              ("dve", dve_xev[i - 2] if (fc == 0 and i >= 2) else 0)],
                       inc="te")
                te_xtr[i] = P.val("te")
                for half in range(2):
                    if i % 2 == 0:
                        ub = ps_u0 if half == 0 else ps_u1
                    else:
                        ub = ps_s0[:, 0:TQB] if half == 0 else ps_s0[:, TQB:2 * TQB]
                    nch = HC // 2
                    OP(V, lambda v, i=i, half=half, ub=ub, nch=nch: v.tensor_copy(
                        out=xq_fm[:].rearrange("p (c t) -> p c t", c=HC)
                            [:, half * nch:(half + 1) * nch, i * 128:(i + 1) * 128],
                        in_=ub[:, 0:nch * 128].rearrange("p (c t) -> p c t", c=nch)),
                       waits=[("te", te_xtr[i])], inc="dve")
                dve_xev[i] = P.val("dve")
            dve_g_done[g_i] = P.val("dve")
            act_g_done[g_i] = P.val("act")
        dve_xq = P.val("dve")
        te_xq = P.val("te")

        sbuf_free("xq_tm", "q16", "xqst", "sq4", "xt4")
        wgu_s = sbuf("wgu_s", [128, 2 * 2 * D], F32)
        wgu_f16 = sbuf("wgu_f16", [128, 2 * D], F16)
        wgu_int = sbuf("wgu_int", [128, 2 * 2 * D], F32)
        wgT = sbuf("wgT", [128, 2 * HC * 128], BF16)
        wuT = sbuf("wuT", [128, 2 * HC * 128], BF16)
        scg = sbuf("scg", [128, 4], F32)
        sgs = sbuf("sgs", [128, 2 * TQB], F32)
        p4b_gate = [("dve", P.val("dve")), ("te", P.val("te")),
                    ("act", P.val("act"))]

        wload_prev = {0: 0, 1: 0}
        dve_q_done = [0, 0]
        te_gu = [0, 0]
        gu_evac = [0, 0]
        act_sil = [0, 0]
        wln = {"wg0": "sgw0", "wg1": "sgw1", "wu0": "sgu0", "wu1": "sgu1"}
        wtn = {"wg0": "st_w0", "wg1": "st_w1", "wu0": "st_u0", "wu1": "st_u1"}

        def prepA(ic):
            r = ic % 2
            sl_w = slice(r * 2 * D, r * 2 * D + D)
            sl_u = slice(r * 2 * D + D, r * 2 * D + 2 * D)
            res = {}
            for wi, (wdram, sl) in enumerate(((wg_d, sl_w), (wu_d, sl_u))):
                key = ("wg" if wi == 0 else "wu") + str(r)
                w = list(p4b_gate) if ic < 2 else [("dve", dve_q_done[r]),
                                                   ("act", act_q_done[r])]
                sdw = OP(SY, lambda s, ic=ic, wdram=wdram, sl=sl: s.dma_start(
                    out=wgu_s[:, sl], in_=wdram[ic * 128:(ic + 1) * 128, :]),
                    waits=w, inc=wln[key])
                res[wi] = sdw
            for wi, sl in ((0, sl_w), (1, sl_u)):
                key = ("wg" if wi == 0 else "wu") + str(r)
                OP(V, lambda v, sl=sl, wi=wi: v.tensor_reduce(
                    out=misc[:, 2 * wi:2 * wi + 1], in_=wgu_s[:, sl], axis=AX.X,
                    op=ALU.max, apply_absolute_value=True),
                   waits=[(wln[key], res[wi]), ("te", wtr_te[r])], inc="dve")
            vdrain()
            for wi in (0, 1):
                OP(V, lambda v, wi=wi: v.reciprocal(
                    out=misc[:, 2 * wi + 1:2 * wi + 2],
                    in_=misc[:, 2 * wi:2 * wi + 1]), inc="dve")
                OP(V, lambda v, wi=wi, r=r: v.tensor_scalar(
                    out=scg[:, 2 * r + wi:2 * r + wi + 1], in0=misc[:, 2 * wi:2 * wi + 1],
                    scalar1=sc_bc[:, 1:2], scalar2=None, op0=ALU.mult),
                   waits=[("act", act_sil[r]), ("dve", gu_evac[r])], inc="dve")
            vdrain()
            for wi in (0, 1):
                OP(V, lambda v, wi=wi: v.tensor_scalar(
                    out=misc[:, 2 * wi + 1:2 * wi + 2], in0=misc[:, 2 * wi + 1:2 * wi + 2],
                    scalar1=127.0, scalar2=None, op0=ALU.mult), inc="dve")
            dvq = P.val("dve")
            for wi, sl in ((0, sl_w), (1, sl_u)):
                OP(S, lambda s, sl=sl, wi=wi: s.activation(
                    out=wgu_f16[:, wi * D:(wi + 1) * D], in_=wgu_s[:, sl],
                    func=AF.Copy, scale=misc[:, 2 * wi + 1:2 * wi + 2], bias=MAGIC),
                   waits=[("dve", dvq), ("te", wtr_te[r])], inc="act")
            sdrain()
            for wi in (0, 1):
                OP(S, lambda s, wi=wi, r=r: s.activation(
                    out=wgu_int[:, (r * 2 + wi) * D:(r * 2 + wi + 1) * D],
                    in_=wgu_f16[:, wi * D:(wi + 1) * D],
                    func=AF.Copy, scale=1.0, bias=-MAGIC), inc="act")
            act_q_done[r] = P.val("act")
            dve_q_done[r] = P.val("dve")

        def prepB(ic):
            r = ic % 2
            for wi, wT in ((0, wgT), (1, wuT)):
                key = ("wg" if wi == 0 else "wu") + str(r)
                nch = HC // 2
                for half in range(2):
                    ub = ps_u0 if half == 0 else ps_u1
                    for fcl in range(nch):
                        fc = half * nch + fcl
                        OP(TE, lambda t, fc=fc, fcl=fcl, ub=ub, wi=wi, r=r: t.matmul(
                            ub[:, fcl * 128:(fcl + 1) * 128],
                            wgu_int[:, (r * 2 + wi) * D + fc * 128:
                                    (r * 2 + wi) * D + (fc + 1) * 128],
                            ident_f32[:], is_transpose=True, start=True, stop=True),
                           waits=[("act", act_q_done[r]) if (half == 0 and fcl == 0) else
                                  ("act", 0),
                                  ("dve", wtr_evac[half]) if fcl == 0 else
                                  ("dve", 0)],
                           inc="te")
                    tev = P.val("te")
                    wtr_evac[half] = OP(V, lambda v, wT=wT, r=r, half=half,
                                                 ub=ub, nch=nch: v.tensor_copy(
                        out=wT[:, r * HC * 128 + half * nch * 128:
                               r * HC * 128 + (half + 1) * nch * 128],
                        in_=ub[:, 0:nch * 128]),
                        waits=[("te", tev)], inc="dve")
            wload_prev_t[key] = 0
            wtr_te[r] = P.val("te")
            wtr_done[r] = P.val("dve")

        def consume(ic, hook=None):
            r = ic % 2
            dv_scg = dve_q_done[r]
            for tb in range(NOB):
                pt = tb % 2
                pg = ps_s0 if pt == 0 else ps_s1
                w = [("dve", wtr_done[r]),
                     ("dve", dve_xq), ("te", te_xq), ("dve", dv_scg),
                     ("act", act_q_done[r]),
                     ("dve", gu_evac[pt]), ("act", act_sil[pt]),
                     ("act", act_last_exp)]
                for ccx in range(HC):
                    OP(TE, lambda t, ccx=ccx, tb=tb, r=r, pg=pg: t.matmul(
                        pg[:, 0:TQB],
                        wgT[:, r * HC * 128 + ccx * 128: r * HC * 128 + (ccx + 1) * 128],
                        xq_fm[:, ccx * TOWN + tb * TQB: ccx * TOWN + (tb + 1) * TQB],
                        start=(ccx == 0), stop=(ccx == HC - 1)), waits=w, inc="te")
                    w = []
                for ccx in range(HC):
                    OP(TE, lambda t, ccx=ccx, tb=tb, r=r, pg=pg: t.matmul(
                        pg[:, TQB:2 * TQB],
                        wuT[:, r * HC * 128 + ccx * 128: r * HC * 128 + (ccx + 1) * 128],
                        xq_fm[:, ccx * TOWN + tb * TQB: ccx * TOWN + (tb + 1) * TQB],
                        start=(ccx == 0), stop=(ccx == HC - 1)), inc="te")
                te_gu[r] = P.val("te")
                tg = te_gu[r]
                if tb == 0 and hook is not None:
                    hook()
                sil = OP(S, lambda s, pt=pt, pg=pg, r=r: s.activation(
                    out=sgs[:, pt * TQB:(pt + 1) * TQB], in_=pg[:, 0:TQB],
                    func=AF.Silu, scale=scg[:, 2 * r:2 * r + 1]),
                    waits=[("te", tg)], inc="act")
                act_sil[pt] = sil
                OP(V, lambda v, ic=ic, tb=tb, pt=pt, pg=pg, r=r: v.scalar_tensor_tensor(
                    out=h_t[:, ic * TOWN + tb * TQB: ic * TOWN + (tb + 1) * TQB],
                    in0=pg[:, TQB:2 * TQB], scalar=scg[:, 2 * r + 1:2 * r + 2],
                    in1=sgs[:, pt * TQB:(pt + 1) * TQB],
                    op0=ALU.mult, op1=ALU.mult), waits=[("act", sil)], inc="dve")
                gu_evac[pt] = P.val("dve")
            vdrain()
            OP(V, lambda v, ic=ic: v.tensor_reduce(
                out=misc[:, 8:9], in_=h_t[:, ic * TOWN:(ic + 1) * TOWN],
                axis=AX.X, op=ALU.max, apply_absolute_value=True), inc="dve")
            vdrain()
            OP(V, lambda v: v.tensor_tensor(out=hmax_acc[:], in0=hmax_acc[:],
                                            in1=misc[:, 8:9], op=ALU.max), inc="dve")

        wload_prev_t = {"wg0": 0, "wg1": 0, "wu0": 0, "wu1": 0}
        te_gu_tb = [0] * NOB
        wtr_evac = [0, 0]
        wtr_done = [0, 0]
        wtr_te = [0, 0]
        act_q_done = [0, 0]
        prepA(0)
        prepB(0)
        if IC > 1:
            prepA(1)
        for ic in range(IC):
            consume(ic, hook=(lambda ic=ic: prepB(ic + 1)) if ic + 1 < IC else None)
            if ic + 2 < IC:
                prepA(ic + 2)
        dve_h = P.val("dve")

        g3 = OP(G, lambda g: g.dma_start(out=cc2_in[:], in_=hmax_acc[:, 0]),
                waits=[("dve", dve_h)], inc="gcc")
        c2 = OP(G, lambda g: g.collective_compute(
            "AllReduce", ALU.max, replica_groups=[list(range(8))],
            ins=[cc2_in.ap()], outs=[cc2_out.ap()]), waits=[("gcc", g3)], inc="cc")
        g4 = OP(G, lambda g: g.dma_start(out=gvec[:], in_=cc2_out.ap()[None, :]),
                waits=[("cc", c2)], inc="gcc")

        sbuf_free("sgs", "scg", "wuT", "wgT", "wgu_int", "wgu_f16", "wgu_s",
                  "xq_fm")
        wdT = sbuf("wdT", [128, HC * IC * 128], F16)
        sqo_s5 = sbuf("sqo_s5", [128, TOWN], F32)
        hq16 = sbuf("hq16", [128, TOWN], F16)
        xoT_t = sbuf("xoT_t", [128, TOWN // 2], F32)
        attn_t = sbuf("attn_t", [128, TOWN // 2], BF16)
        out_st = sbuf("out_st", [128, TOWN // 2], F32)
        wd_s = sbuf("wd_s", [128, I], F32)
        wd_f16 = sbuf("wd_f16", [128, I], F16)
        te_p4 = P.val("te")
        p5_gate = [("dve", P.val("dve")), ("te", te_p4),
                   ("act", P.val("act")),
                   ("sx0", P.val("sx0")), ("sx1", P.val("sx1")),
                   ("st_w0", P.val("st_w0")), ("st_w1", P.val("st_w1")),
                   ("st_u0", P.val("st_u0")), ("st_u1", P.val("st_u1")),
                   ("sgw0", P.val("sgw0")), ("sgw1", P.val("sgw1")),
                   ("sgu0", P.val("sgu0")), ("sgu1", P.val("sgu1"))]

        WSC = 2.0 ** 14
        TBW = TOWN // 2
        NGQ = IC // 2
        act_hq_tbp = [0, 0]
        hq_state = {"next": 0, "first": True, "p3": 0}

        def hview(ic0, tbp):
            return (h_t[:].rearrange("p (i b q) -> p i b q", i=IC, b=2)
                    [:, ic0:ic0 + 2, tbp, :])

        def emit_hq(n):
            for _ in range(n):
                gi = hq_state["next"]
                if gi >= 2 * NGQ:
                    return
                hq_state["next"] += 1
                tbp, g = divmod(gi, NGQ)
                ic0 = 2 * g
                w1 = [("te", te_p4)] if hq_state["first"] else []
                hq_state["first"] = False
                OP(V, lambda v, ic0=ic0, tbp=tbp: v.tensor_scalar(
                    out=sqo_s5[:].rearrange("p (a q) -> p a q", a=2),
                    in0=hview(ic0, tbp),
                    scalar1=sc_bc[:, 3:4], scalar2=127.0,
                    op0=ALU.mult, op1=ALU.min), waits=w1, inc="dve")
                vdrain()
                w2 = ([("act", hq_state["p3"])] if gi > 0 else [("te", te_p4)])
                OP(V, lambda v: v.tensor_scalar(
                    out=hq16[:, 0:TOWN], in0=sqo_s5[:, 0:TOWN],
                    scalar1=-128.0, scalar2=MAGIC,
                    op0=ALU.max, op1=ALU.add), waits=w2, inc="dve")
                p2v = P.val("dve")
                vdrain()
                OP(S, lambda s, ic0=ic0, tbp=tbp: s.activation(
                    out=hview(ic0, tbp),
                    in_=hq16[:].rearrange("p (a q) -> p a q", a=2),
                    func=AF.Copy, scale=1.0 / WSC, bias=-MAGIC / WSC),
                    waits=[("dve", p2v)], inc="act")
                hq_state["p3"] = P.val("act")
                if g == NGQ - 1:
                    act_hq_tbp[tbp] = P.val("act")

        def emit_cc2_tail():
            OP(V, lambda v: v.tensor_reduce(out=scal[0:1, 16:17], in_=gvec[:], axis=AX.X,
                                            op=ALU.max), waits=[("gcc", g4)], inc="dve")
            vdrain()
            OP(V, lambda v: v.scalar_tensor_tensor(out=scal[0:1, 17:18], in0=scal[0:1, 16:17],
                                                   scalar=1.0 / 127.0, in1=scal[0:1, 3:4],
                                                   op0=ALU.mult, op1=ALU.mult), inc="dve")
            vdrain()
            OP(V, lambda v: v.reciprocal(out=scal[0:1, 18:19], in_=scal[0:1, 17:18]), inc="dve")
            dvsh = P.val("dve")
            tbc = OP(TE, lambda t: t.matmul(ps_u1[:, 0:2], ones128f[:], scal[0:1, 17:19],
                                            start=True, stop=True),
                     waits=[("dve", dvsh)], inc="te")
            OP(V, lambda v: v.tensor_copy(out=sc_bc[:, 2:4], in_=ps_u1[:, 0:2]),
               waits=[("te", tbc)], inc="dve")
            dve_sh = P.val("dve")

        wd_ev_co = [0] * HC
        fin_prev = [0]
        wd_te_prev = [0]
        ev_ping = [0, 0]

        def emit_wd(co, mid_hook=None):
            r = co % 2
            sdn = "swd0" if r == 0 else "swd1"
            w = (list(p5_gate) if co == 0 else
                 [("dve", fin_prev[0]), ("te", wd_te_prev[0])])
            sdw = OP(SY, lambda s, co=co: s.dma_start(
                out=wd_s[:], in_=wd_d[co * 128:(co + 1) * 128, :]), waits=w, inc=sdn)
            OP(V, lambda v: v.tensor_reduce(
                out=misc[:, 4:5], in_=wd_s[:], axis=AX.X,
                op=ALU.max, apply_absolute_value=True),
               waits=[(sdn, sdw)], inc="dve")
            vdrain()
            OP(V, lambda v: v.reciprocal(out=misc[:, 5:6], in_=misc[:, 4:5]), inc="dve")
            OP(V, lambda v: v.tensor_scalar(out=misc[:, 6:7], in0=misc[:, 4:5],
                                            scalar1=1.0 / 127.0, scalar2=None,
                                            op0=ALU.mult), inc="dve")
            vdrain()
            OP(V, lambda v: v.tensor_scalar(out=misc[:, 5:6], in0=misc[:, 5:6],
                                            scalar1=127.0, scalar2=None, op0=ALU.mult),
               inc="dve")
            if mid_hook is not None:
                mid_hook()
                vdrain()
            OP(V, lambda v: v.tensor_scalar(out=misc[:, 7:8], in0=misc[:, 6:7],
                                            scalar1=sc_bc[:, 2:3], scalar2=WSC,
                                            op0=ALU.mult, op1=ALU.mult), inc="dve")
            dvq = P.val("dve")
            vdrain()
            aq = OP(S, lambda s: s.activation(
                out=wd_f16[:], in_=wd_s[:], func=AF.Copy,
                scale=misc[:, 5:6], bias=MAGIC), waits=[("dve", dvq)], inc="act")
            fw = [("act", aq)]
            if co == 0:
                fw.append(("te", te_p4))
            OP(V, lambda v: v.tensor_scalar(
                out=wd_s[:], in0=wd_f16[:],
                scalar1=-MAGIC, scalar2=misc[:, 7:8],
                op0=ALU.add, op1=ALU.mult), waits=fw, inc="dve")
            fin = P.val("dve")
            fin_prev[0] = fin
            vdrain()
            NGW = (IC + 3) // 4
            for g in range(NGW):
                k0 = g * 4
                nk = min(4, IC - k0)
                pb = ps_u0 if g % 2 == 0 else ps_u1
                tw = [("dve", fin), ("act", ev_ping[g % 2])]
                for k in range(nk):
                    OP(TE, lambda t, k=k, k0=k0, pb=pb: t.matmul(
                        pb[:, k * 128:(k + 1) * 128],
                        wd_s[:, (k0 + k) * 128:(k0 + k + 1) * 128],
                        ident_f32[:], is_transpose=True,
                        start=(k == 0), stop=(k == nk - 1)),
                       waits=tw, inc="te")
                    tw = []
                tg = P.val("te")
                OP(S, lambda s, g=g, k0=k0, nk=nk, co=co, pb=pb: s.activation(
                    out=wdT[:, (co * IC + k0) * 128:(co * IC + k0 + nk) * 128],
                    in_=pb[:, 0:nk * 128], func=AF.Copy,
                    scale=1.0, bias=0.0), waits=[("te", tg)], inc="act")
                ev_ping[g % 2] = P.val("act")
            wd_te_prev[0] = P.val("te")
            wd_ev_co[co] = P.val("act")

        NS8 = TBW // 128
        te_xp = [0]
        out_evac = [0, 0]
        ev_prev = [0]
        store_prev = [0]

        tile_seq = [0]

        def emit_tiles(co, tbp):
            if True:
                ti = tile_seq[0]
                tile_seq[0] += 1
                rr = ti % 2
                ub = ps_s0 if rr == 0 else ps_s1
                sds = 0
                for s8 in range(NS8):
                    xw = [("te", te_xp[0])] if ti >= 1 else list(p5_gate)
                    sds = OP("scalar", lambda s, co=co, tbp=tbp, s8=s8: s.dma_start(
                        out=xoT_t[:, s8 * 128:(s8 + 1) * 128],
                        in_=x_own[tbp * TBW + s8 * 128: tbp * TBW + (s8 + 1) * 128,
                                  co * 128:(co + 1) * 128]), waits=xw, inc="sx5a")
                sa = OP(G, lambda g, co=co, tbp=tbp: g.dma_start(
                    out=attn_t[:, 0:TBW],
                    in_=attn_d.ap()[co * 128:(co + 1) * 128,
                                    tbp * TBW:(tbp + 1) * TBW]),
                    waits=[("cc", c3), ("dve", ev_prev[0])] +
                          ([] if ti >= 1 else p5_gate), inc="sat0")
                w = [("act", wd_ev_co[co]), ("sx5a", sds),
                     ("dve", out_evac[rr])]
                NSQ = TQB // 128
                for s8 in range(NS8):
                    OP(TE, lambda t, s8=s8, ub=ub: t.matmul(
                        ub[:, s8 * 128:(s8 + 1) * 128],
                        xoT_t[:, s8 * 128:(s8 + 1) * 128],
                        ident_f32[:], is_transpose=True,
                        start=(s8 % NSQ == 0), stop=False),
                       waits=w, inc="te")
                    w = []
                te_xp[0] = P.val("te")
                NHF = TBW // TQB
                for icx in range(IC):
                    for hf in range(NHF):
                        mw = ([("act", act_hq_tbp[tbp])]
                              if icx == 0 and hf == 0 else [])
                        OP(TE, lambda t, icx=icx, co=co, tbp=tbp, hf=hf, ub=ub:
                           t.matmul(
                            ub[:, hf * TQB:(hf + 1) * TQB],
                            wdT[:, (co * IC + icx) * 128:(co * IC + icx + 1) * 128],
                            h_t[:, icx * TOWN + tbp * TBW + hf * TQB:
                                icx * TOWN + tbp * TBW + (hf + 1) * TQB],
                            start=False, stop=(icx == IC - 1)),
                           waits=mw, inc="te")
                tt = P.val("te")
                ev = OP(V, lambda v, ub=ub: v.tensor_tensor(
                    out=out_st[:, 0:TBW], in0=ub[:, 0:TBW],
                    in1=attn_t[:, 0:TBW], op=ALU.add),
                    waits=[("te", tt), ("sat0", sa), ("so0", store_prev[0])], inc="dve")
                ev_prev[0] = ev
                out_evac[rr] = ev
                store_prev[0] = OP(G, lambda g, co=co, tbp=tbp: g.dma_start(
                    out=out_d[co * 128:(co + 1) * 128, tbp * TBW:(tbp + 1) * TBW],
                    in_=out_st[:, 0:TBW]), waits=[("dve", ev)], inc="so0")

        hpc = max(1, (2 * NGQ + HC - 1) // HC)
        lead = (NGQ + hpc - 1) // hpc
        for co in range(HC):
            emit_wd(co, mid_hook=emit_cc2_tail if co == 0 else None)
            emit_hq(hpc)
            if co >= lead:
                emit_tiles(co - lead, 0)
        emit_hq(2 * NGQ)
        for co in range(HC - lead, HC):
            emit_tiles(co, 0)
        for co in range(HC):
            emit_tiles(co, 1)
        OP(G, lambda g: None, waits=[("so0", P.val("so0"))])

        def runner(eng_name):
            def run(eng):
                for fn, w, inc, amt in P.ops[eng_name]:
                    for s_, val in w:
                        eng.wait_ge(sem_h[s_], val)
                    ins = fn(eng)
                    if inc is not None:
                        assert ins is not None, f"{eng_name} op with inc returned None"
                        ins.then_inc(sem_h[inc], amt)
            return run

        block.sync(runner("sync"))
        block.gpsimd(runner("gpsimd"))
        block.vector(runner("vector"))
        block.scalar(runner("scalar"))
        block.tensor(runner("tensor"))

    return nc, P


def shard_inputs(cfg: Cfg, inp):
    x = np.ascontiguousarray(np.asarray(inp["L0_attn_x"], dtype=np.float32))
    W_attn = np.asarray(inp["L0_attn_W_attn"], dtype=np.float32)
    b_attn = np.asarray(inp["L0_attn_b_attn"], dtype=np.float32)
    W_proj = np.asarray(inp["L0_attn_W_proj"], dtype=np.float32)
    b_proj = np.asarray(inp["L0_attn_b_proj"], dtype=np.float32)
    D, HPC = cfg.D, cfg.HPC
    in_maps = []
    for c in range(8):
        b, h = c // 2, c % 2
        hs = slice(h * HPC * 64, (h + 1) * HPC * 64)
        wq = W_attn[:, 0 * D:1 * D][:, hs]
        wk = W_attn[:, 1 * D:2 * D][:, hs]
        wv = W_attn[:, 2 * D:3 * D][:, hs]
        wqkv = np.ascontiguousarray(np.concatenate([wq, wk, wv], axis=1))
        bq = np.concatenate([b_attn[0 * D:1 * D][hs], b_attn[1 * D:2 * D][hs],
                             b_attn[2 * D:3 * D][hs]])
        in_maps.append({
            "x_full": np.ascontiguousarray(x[b]),
            "x_own": np.ascontiguousarray(x[b, h * cfg.TOWN:(h + 1) * cfg.TOWN]),
            "wqkv": wqkv,
            "bqkv": np.ascontiguousarray(bq),
            "wproj": np.ascontiguousarray(W_proj[hs, :]),
            "bproj": np.ascontiguousarray(b_proj),
            "clip_x": np.asarray(inp["L0_ffn_clip_x"], dtype=np.float32),
            "clip_down": np.asarray(inp["L0_ffn_clip_down"], dtype=np.float32),
            "wg": np.ascontiguousarray(np.asarray(inp["L0_ffn_w_gate"], dtype=np.float32)),
            "wu": np.ascontiguousarray(np.asarray(inp["L0_ffn_w_up"], dtype=np.float32)),
            "wd": np.ascontiguousarray(np.asarray(inp["L0_ffn_w_down"], dtype=np.float32)),
        })
    return in_maps


def assemble(cfg: Cfg, results):
    out = np.empty((cfg.NB, cfg.T, cfg.D), dtype=np.float32)
    for c in range(8):
        b, h = c // 2, c % 2
        out[b, h * cfg.TOWN:(h + 1) * cfg.TOWN, :] = np.asarray(results[c]["out"]).T
    return out




def kernel(**inputs):
    cfg = Cfg()
    in_maps = shard_inputs(cfg, inputs)
    nc, _ = build_program(cfg)
    from concourse.bass_utils import run_bass_kernel_spmd
    res = run_bass_kernel_spmd(nc, in_maps, core_ids=list(range(8)))
    return assemble(cfg, res.results)


# version 26 (final)
# speedup vs baseline: 1.0964x; 1.0964x over previous
import sys
from contextlib import ExitStack
from dataclasses import dataclass
import numpy as np
import concourse.bass as bass
import concourse.mybir as mybir

F32 = mybir.dt.float32
BF16 = mybir.dt.bfloat16
F16 = mybir.dt.float16
AF = mybir.ActivationFunctionType
ALU = mybir.AluOpType
AX = mybir.AxisListType

NEG_BIG = -1.0e30
MAGIC = 1536.0


@dataclass(frozen=True)
class Cfg:
    T: int = 4096
    D: int = 1024
    HPC: int = 8
    I: int = 2816
    TQB: int = 512
    NB: int = 4
    DH: int = 64

    @property
    def HC(self): return self.D // 128
    @property
    def IC(self): return self.I // 128
    @property
    def NTK(self): return self.T // 128
    @property
    def NQB(self): return self.T // self.TQB
    @property
    def TOWN(self): return self.T // 2
    @property
    def NOB(self): return self.TOWN // self.TQB
    @property
    def PAIRS(self): return self.HPC // 2
    @property
    def DIAG(self): return self.TQB // 128


class Prog:
    ENGINES = ("sync", "gpsimd", "vector", "scalar", "tensor")

    def __init__(self):
        self.ops = {e: [] for e in self.ENGINES}
        self.cnt = {}
        self.last_wait = {e: {} for e in self.ENGINES}

    def val(self, sem):
        return self.cnt.get(sem, 0)

    def op(self, eng, fn, waits=(), inc=None, inc_amt=None):
        if inc_amt is None:
            inc_amt = 1 if inc in ("cc", "te", "dve", "act", "gps") else 16
        w = []
        for s, v in waits:
            if not v or v <= 0:
                continue
            if v > self.last_wait[eng].get(s, 0):
                self.last_wait[eng][s] = v
                w.append((s, v))
        if inc is not None:
            self.cnt[inc] = self.cnt.get(inc, 0) + inc_amt
        self.ops[eng].append((fn, w, inc, inc_amt))
        return self.cnt.get(inc)


DEBUG_SYNC = False


def build_program(cfg: Cfg):
    nc = bass.Bass("TRN2", target_bir_lowering=False)
    T, D, HPC, I, TQB = cfg.T, cfg.D, cfg.HPC, cfg.I, cfg.TQB
    HC, IC, NTK, NQB, TOWN, NOB, PAIRS, DIAG = (
        cfg.HC, cfg.IC, cfg.NTK, cfg.NQB, cfg.TOWN, cfg.NOB, cfg.PAIRS, cfg.DIAG)
    NT = T // 128
    NT_OWN = TOWN // 128
    G4 = 4 if NT % 4 == 0 else 2
    NGR = NT // G4
    NGR_OWN = NT_OWN // G4

    x_full = nc.dram_tensor("x_full", [T, D], F32, kind="ExternalInput").ap()
    x_own = nc.dram_tensor("x_own", [TOWN, D], F32, kind="ExternalInput").ap()
    wqkv = nc.dram_tensor("wqkv", [D, 3 * HPC * 64], F32, kind="ExternalInput").ap()
    bqkv = nc.dram_tensor("bqkv", [3 * HPC * 64], F32, kind="ExternalInput").ap()
    wproj = nc.dram_tensor("wproj", [HPC * 64, D], F32, kind="ExternalInput").ap()
    bproj = nc.dram_tensor("bproj", [D], F32, kind="ExternalInput").ap()
    clip_x = nc.dram_tensor("clip_x", [1], F32, kind="ExternalInput").ap()
    clip_down = nc.dram_tensor("clip_down", [1], F32, kind="ExternalInput").ap()
    wg_d = nc.dram_tensor("wg", [I, D], F32, kind="ExternalInput").ap()
    wu_d = nc.dram_tensor("wu", [I, D], F32, kind="ExternalInput").ap()
    wd_d = nc.dram_tensor("wd", [D, I], F32, kind="ExternalInput").ap()
    out_d = nc.dram_tensor("out", [D, TOWN], F32, kind="ExternalOutput").ap()
    cc1_in = nc.dram_tensor("cc1_in", [128], F32)
    cc1_out = nc.dram_tensor("cc1_out", [128], F32)
    cc2_in = nc.dram_tensor("cc2_in", [128], F32)
    cc2_out = nc.dram_tensor("cc2_out", [128], F32)
    po_d = nc.dram_tensor("po_d", [2, D, TOWN], BF16)
    attn_d = nc.dram_tensor("attn_d", [D, TOWN], BF16)

    P = Prog()
    st = ExitStack()
    _open = []

    def sbuf(name, shape, dt=F32):
        cm = nc.sbuf_tensor(name, shape, dt)
        t = cm.__enter__()
        _open.append((name, cm))
        return t

    def sbuf_free(*names):
        for name in names:
            nm, cm = _open.pop()
            assert nm == name, f"free order mismatch: {nm} != {name}"
            cm.__exit__(None, None, None)

    with st:
        ident_f32 = sbuf("ident_f32", [128, 128], F32)
        ident_bf = sbuf("ident_bf", [128, 128], BF16)
        masks_bf = sbuf("masks_bf", [128, DIAG * TQB], BF16)
        ones_row = sbuf("ones_row", [1, TQB], BF16)
        half_row = sbuf("half_row", [1, TQB], BF16)
        ones64f = sbuf("ones64f", [1, 64], F32)
        ones_p64 = sbuf("ones_p64", [65, 64], F32)
        ones128f = sbuf("ones128f", [1, 128], F32)
        slide = sbuf("slide", [128, NQB + 15], BF16)
        eps_b = sbuf("eps_b", [128, 1], F32)
        st1 = sbuf("st1", [128, 64], F32)
        xmax_acc = sbuf("xmax_acc", [128, 1], F32)
        hmax_acc = sbuf("hmax_acc", [128, 1], F32)
        gvec = sbuf("gvec", [1, 128], F32)
        scal = sbuf("scal", [1, 32], F32)
        sc_bc = sbuf("sc_bc", [128, 8], F32)
        n16 = sbuf("n16", [16, 1], F32)
        n16r = sbuf("n16r", [1, 16], F32)
        negM = sbuf("negM", [128, 2], F32)
        den128 = sbuf("den128", [128, max(2 * TQB // 128, 1)], F32)
        misc = sbuf("misc", [128, 16], F32)
        xn_fm = sbuf("xn_fm", [128, HC * T], BF16)
        wpair_b = sbuf("wpair_b", [128, HC * 384], BF16)
        waug_b = sbuf("waug_b", [128, HC * 130], BF16)
        bq_sb = sbuf("bq_sb", [1, 384], BF16)
        bvaug = sbuf("bvaug", [1, 130], BF16)
        k_pair = sbuf("k_pair", [128, T], BF16)
        q_pair = sbuf("q_pair", [128, T], BF16)
        v_aug = sbuf("v_aug", [128, NTK * 130], BF16)
        sqs = sbuf("sqs", [128, 2 * TQB], BF16)
        p_pair = sbuf("p_pair", [128, 2 * 2 * TQB], BF16)
        denrow = sbuf("denrow", [65, 2 * TQB], F32)
        wproj_b = sbuf("wproj_b", [64, HPC * D], BF16)
        bproj_b = sbuf("bproj_b", [1, D], BF16)
        po_st = sbuf("po_st", [128, 2 * TQB], BF16)
        denr = denrow[0:1, :]
        NXR = 2 * G4
        xt = sbuf("xt", [128, NXR * D], F32)
        sq_s = sbuf("sq_s", [128, D], F32)
        xnf = sbuf("xnf", [128, 2 * D], F32)

        ps_s0 = st.enter_context(nc.psum_tensor("ps_s0", [128, 2 * TQB], F32))
        ps_s1 = st.enter_context(nc.psum_tensor("ps_s1", [128, 2 * TQB], F32))
        ps_avA = st.enter_context(nc.psum_tensor("ps_avA", [128, TQB], F32))
        ps_avB = st.enter_context(nc.psum_tensor("ps_avB", [128, TQB], F32))
        ps_u0 = st.enter_context(nc.psum_tensor("ps_u0", [128, TQB], F32))
        ps_u1 = st.enter_context(nc.psum_tensor("ps_u1", [128, TQB], F32))

        SEM_NAMES = ("sclip", "sx0", "sx1", "swq0", "sbq", "swp0", "sbp",
                     "sgw0", "sgw1", "sgu0", "sgu1", "swd0", "swd1",
                     "sx5a", "sx5b", "sat0", "sat1",
                     "st_w0", "st_w1", "st_u0", "st_u1", "st_d0", "st_d1",
                     "gden", "gcc", "spo0", "spo1", "so0", "so1",
                     "cc", "te", "dve", "act", "gps")
        sem_h = {name: st.enter_context(nc.semaphore(name)) for name in SEM_NAMES}
        block = st.enter_context(nc.Block())

        def OP(eng, fn, waits=(), inc=None, amt=None):
            return P.op(eng, fn, waits=waits, inc=inc, inc_amt=amt)

        V = "vector"; S = "scalar"; TE = "tensor"; G = "gpsimd"; SY = "sync"

        def vdrain():
            OP(V, lambda v: v.drain())

        def sdrain():
            OP(S, lambda s: s.drain())

        OP(G, lambda g: g.memset(ident_f32[:], 0.0), inc="gps")
        OP(G, lambda g: g.memset(masks_bf[:], 0.0), inc="gps")
        OP(G, lambda g: g.memset(slide[:], 0.0), inc="gps")
        OP(G, lambda g: g.affine_select(out=ident_f32[:], in_=ident_f32[:],
                                        compare_op=ALU.not_equal, fill=1.0, base=0,
                                        pattern=[[-1, 128]], channel_multiplier=1),
           waits=[("gps", 3)], inc="gps")

        def mk_mask(dlt):
            def f(g):
                return g.affine_select(
                    out=masks_bf[:, dlt * TQB:(dlt + 1) * TQB],
                    in_=masks_bf[:, dlt * TQB:(dlt + 1) * TQB],
                    compare_op=ALU.is_ge, fill=NEG_BIG,
                    base=-128 * dlt, pattern=[[1, TQB]], channel_multiplier=-1)
            return f
        for dlt in range(DIAG):
            OP(G, mk_mask(dlt), inc="gps")
        OP(G, lambda g: g.affine_select(out=slide[:], in_=slide[:],
                                        compare_op=ALU.not_equal, fill=1.0,
                                        base=NQB - 1, pattern=[[-1, NQB + 15]],
                                        channel_multiplier=0),
           waits=[("gps", 4 + DIAG)], inc="gps")
        gps_const = P.val("gps")
        OP(V, lambda v: v.memset(ones_row[:], 1.0), inc="dve")
        OP(V, lambda v: v.memset(half_row[:], 0.5), inc="dve")
        OP(V, lambda v: v.memset(ones64f[:], 1.0), inc="dve")
        OP(V, lambda v: v.memset(ones_p64[64:65, :], 1.0), inc="dve")
        OP(V, lambda v: v.memset(ones128f[:], 1.0), inc="dve")
        OP(V, lambda v: v.memset(eps_b[:], 1e-5), inc="dve")
        OP(V, lambda v: v.memset(xmax_acc[:], 0.0), inc="dve")
        OP(V, lambda v: v.memset(hmax_acc[:], 0.0), inc="dve")
        OP(V, lambda v: v.tensor_copy(out=ident_bf[:], in_=ident_f32[:]),
           waits=[("gps", gps_const)], inc="dve")
        dve_const = P.val("dve")

        OP(SY, lambda s: s.dma_start(out=scal[0:1, 0:1], in_=clip_x[None, :]), inc="sclip")
        sd = OP(SY, lambda s: s.dma_start(out=scal[0:1, 1:2], in_=clip_down[None, :]), inc="sclip")
        act_sig = OP(S, lambda s: s.activation(out=scal[0:1, 2:4], in_=scal[0:1, 0:2],
                                               func=AF.Sigmoid), waits=[("sclip", sd)], inc="act")

        te_pair_prev = 0
        act_exp_par = {0: 0, 1: 0}
        act_last_exp = 0
        dve_y_evac_prev = 0
        v_evac = [0, 0]

        def emit_pair_weights(pr, extra_waits):
            wq_ld = 0
            for ccx in range(HC):
                wq_ld = OP(G, lambda g, ccx=ccx, pr=pr: g.dma_start(
                    out=wpair_b[:, ccx * 384:(ccx + 1) * 384]
                        .rearrange("p (s c) -> p s c", s=3),
                    in_=wqkv[ccx * 128:(ccx + 1) * 128, :]
                        .rearrange("p (s hc) -> p s hc", s=3)[:, :, pr * 128:(pr + 1) * 128]),
                    waits=extra_waits, inc="swq0")
            OP(V, lambda v: v.memset(waug_b[:], 0.0), waits=extra_waits, inc="dve")
            vdrain()
            for ccx in range(HC):
                OP(V, lambda v, ccx=ccx: v.tensor_copy(
                    out=waug_b[:, ccx * 130:(ccx + 1) * 130]
                        .rearrange("p (g c) -> p g c", g=2, c=65)[:, :, 0:64],
                    in_=wpair_b[:, ccx * 384 + 256: ccx * 384 + 384]
                        .rearrange("p (g c) -> p g c", g=2, c=64)),
                   waits=[("swq0", wq_ld)], inc="dve")
            bq_ld = OP(G, lambda g, pr=pr: g.dma_start(
                out=bq_sb[:].rearrange("one (s c) -> one s c", s=3),
                in_=bqkv[None, :].rearrange("one (s hc) -> one s hc", s=3)
                    [:, :, pr * 128:(pr + 1) * 128]),
                waits=extra_waits, inc="sbq")
            OP(V, lambda v: v.memset(bvaug[:], 1.0), waits=extra_waits, inc="dve")
            vdrain()
            OP(V, lambda v: v.tensor_copy(
                out=bvaug[:].rearrange("one (g c) -> one g c", g=2, c=65)[:, :, 0:64],
                in_=bq_sb[0:1, 256:384].rearrange("one (g c) -> one g c", g=2, c=64)),
               waits=[("sbq", bq_ld)], inc="dve")
            return {"wq_ld": wq_ld, "dve_wp": P.val("dve"),
                    "bankfree": [0, 0], "bseq": 0,
                    "sqs_te": [0, 0], "pending_norm": [],
                    "v_i": 0}

        def emit_kq_tb(stt, which, tb, banks, extra_waits=()):
            dst, sec, psoff = (q_pair, 0, 0) if which == "q" else (k_pair, 1, TQB)
            bk = stt["bseq"] % 2
            stt["bseq"] += 1
            ub = banks[bk]
            w = [("dve", stt["dve_wp"]), ("dve", stt["bankfree"][bk]),
                 ("swq0", stt["wq_ld"]), ("gps", gps_const)] + list(extra_waits)
            for ccx in range(HC):
                OP(TE, lambda t, ccx=ccx, sec=sec, tb=tb, ub=ub: t.matmul(
                    ub[:, 0:TQB],
                    wpair_b[:, ccx * 384 + sec * 128: ccx * 384 + sec * 128 + 128],
                    xn_fm[:, ccx * T + tb * TQB: ccx * T + (tb + 1) * TQB],
                    start=(ccx == 0), stop=False), waits=w, inc="te")
                w = []
            tb_mm = OP(TE, lambda t, sec=sec, ub=ub: t.matmul(
                ub[:, 0:TQB], bq_sb[0:1, sec * 128:sec * 128 + 128],
                ones_row[:], start=False, stop=True), inc="te")
            if stt["pending_norm"]:
                stt["pending_norm"].pop(0)()
            ev = OP(V, lambda v, dst=dst, tb=tb, ub=ub: v.tensor_copy(
                out=dst[:, tb * TQB:(tb + 1) * TQB], in_=ub[:, 0:TQB]),
                waits=[("te", tb_mm)], inc="dve")
            vdrain()
            OP(V, lambda v, dst=dst, tb=tb: v.scalar_tensor_tensor(
                out=sqs[:, (tb % 2) * TQB:(tb % 2 + 1) * TQB],
                in0=dst[:, tb * TQB:(tb + 1) * TQB], scalar=1.0,
                in1=dst[:, tb * TQB:(tb + 1) * TQB],
                op0=ALU.mult, op1=ALU.mult),
               waits=[("te", stt["sqs_te"][tb % 2])], inc="dve")
            stt["bankfree"][bk] = P.val("dve")
            dvn = P.val("dve")

            def norm_emit(tb=tb, psoff=psoff, dvn=dvn):
                def doit():
                    r_ = OP(TE, lambda t: t.matmul(
                        ps_s1[0:16, psoff:psoff + TQB],
                        slide[:, NQB - 1 - tb: NQB - 1 - tb + 16],
                        sqs[:, (tb % 2) * TQB:(tb % 2 + 1) * TQB],
                        start=(tb == 0), stop=(tb == NQB - 1)),
                       waits=[("dve", dvn), ("act", act_last_exp)], inc="te")
                    stt["sqs_te"][tb % 2] = r_
                return doit
            stt["pending_norm"].append(norm_emit())

        def emit_v_batch(stt, banks, nbmax, extra_waits=()):
            vt = stt["v_i"]
            nb3 = min(nbmax, NTK - vt, max(1, TQB // 130))
            if nb3 <= 0:
                return
            bk = stt["bseq"] % 2
            stt["bseq"] += 1
            ub = banks[bk]
            w = [("dve", stt["dve_wp"]), ("dve", stt["bankfree"][bk])] + list(extra_waits)
            for b3 in range(nb3):
                for ccx in range(HC):
                    OP(TE, lambda t, vt=vt, b3=b3, ccx=ccx, ub=ub: t.matmul(
                        ub[:, b3 * 130:(b3 + 1) * 130],
                        xn_fm[:, ccx * T + (vt + b3) * 128: ccx * T + (vt + b3 + 1) * 128],
                        waug_b[:, ccx * 130:(ccx + 1) * 130],
                        start=(ccx == 0), stop=False), waits=w, inc="te")
                    w = []
                OP(TE, lambda t, b3=b3, ub=ub: t.matmul(
                    ub[:, b3 * 130:(b3 + 1) * 130], ones_row[0:1, 0:128],
                    bvaug[:], start=False, stop=True), inc="te")
            tv = P.val("te")
            ev_ = OP(V, lambda v, vt=vt, nb3=nb3, ub=ub: v.tensor_copy(
                out=v_aug[:, vt * 130:(vt + nb3) * 130], in_=ub[:, 0:nb3 * 130]),
                waits=[("te", tv)], inc="dve")
            stt["bankfree"][bk] = ev_
            v_evac[bk] = ev_
            stt["v_i"] = vt + nb3

        def emit_norm_tail(stt, parP):
            while stt["pending_norm"]:
                stt["pending_norm"].pop(0)()
            te_norm = P.val("te")
            OP(V, lambda v: v.tensor_reduce(out=n16[:, 0:1], in_=ps_s1[0:16, 0:TQB],
                                            axis=AX.X, op=ALU.max),
               waits=[("te", te_norm)], inc="dve")
            OP(V, lambda v: v.tensor_reduce(out=misc[0:16, 0:1],
                                            in_=ps_s1[0:16, TQB:2 * TQB],
                                            axis=AX.X, op=ALU.max), inc="dve")
            vdrain()
            OP(V, lambda v: v.tensor_tensor(out=n16[:, 0:1], in0=n16[:, 0:1],
                                            in1=misc[0:16, 0:1], op=ALU.max), inc="dve")
            dvn16 = P.val("dve")
            gn = OP(G, lambda g: g.dma_start(out=n16r[:], in_=n16[:, 0:1]),
                    waits=[("dve", dvn16)], inc="gden")
            OP(V, lambda v: v.tensor_reduce(out=scal[0:1, 10:11], in_=n16r[:],
                                            axis=AX.X, op=ALU.max),
               waits=[("gden", gn)], inc="dve")
            vdrain()
            OP(V, lambda v: v.tensor_scalar(out=scal[0:1, 11:12], in0=scal[0:1, 10:11],
                                            scalar1=-0.125, scalar2=None, op0=ALU.mult),
               inc="dve")
            dvm = P.val("dve")
            tm = OP(TE, lambda t: t.matmul(ps_u1[:, 0:1], ones128f[:], scal[0:1, 11:12],
                                           start=True, stop=True),
                    waits=[("dve", dvm), ("dve", stt["bankfree"][1])], inc="te")
            OP(V, lambda v, parP=parP: v.tensor_copy(out=negM[:, parP:parP + 1],
                                                     in_=ps_u1[:, 0:1]),
               waits=[("te", tm), ("act", act_exp_par[0]), ("act", act_exp_par[1])],
               inc="dve")
            return P.val("dve")

        pair_state = emit_pair_weights(0, [("gps", gps_const)])

        def p1_hook(g_i):
            if g_i >= NQB:
                return
            ew = [("dve", dve_ev_done[min(g_i * G4 + G4 - 1, NT - 1)])]
            emit_kq_tb(pair_state, "q", g_i, (ps_avA, ps_avB), ew)
            emit_kq_tb(pair_state, "k", g_i, (ps_avA, ps_avB), ew)
            while pair_state["v_i"] < min((g_i + 1) * G4, NTK):
                emit_v_batch(pair_state, (ps_avA, ps_avB), 2, ew)

        P._p1_hook = p1_hook

        dve_tile_done = [0] * NT
        te_tr_done = [0] * NT
        dve_ev_done = [0] * NT
        act_grp_done = [0] * NGR
        dve_grp_done = [0] * NGR
        for g_i in range(NGR):
            gp = g_i % 2
            sb32 = gp * 32
            sxn = "sx0" if gp == 0 else "sx1"
            sds = [0] * G4
            for k in range(G4):
                i = g_i * G4 + k
                slot = (i % NXR)
                xsl = xt[:, slot * D:(slot + 1) * D]
                w = []
                if g_i >= 2:
                    w = [("dve", dve_grp_done[g_i - 2]), ("act", act_grp_done[g_i - 2])]
                sds[k] = OP(SY, lambda s, i=i, xsl=xsl: s.dma_start(
                    out=xsl, in_=x_full[i * 128:(i + 1) * 128, :]), waits=w, inc=sxn)
            for k in range(G4):
                i = g_i * G4 + k
                slot = (i % NXR)
                xsl = xt[:, slot * D:(slot + 1) * D]
                sqsl = sq_s[:, 0:D]
                OP(V, lambda v, xsl=xsl, c=sb32 + k: v.tensor_reduce(
                    out=st1[:, c:c + 1], in_=xsl, axis=AX.X, op=ALU.add),
                   waits=[(sxn, sds[G4 - 1])], inc="dve")
                sdrain()
                OP(S, lambda s, xsl=xsl, sqsl=sqsl, c=sb32 + 4 + k: s.activation(
                    out=sqsl, in_=xsl, func=AF.Square, accum_out=st1[:, c:c + 1]),
                   waits=[(sxn, sds[G4 - 1]),
                          ("dve", dve_grp_done[g_i - 2] if g_i >= 2 else 0)], inc="act")
            a_sq = P.val("act")
            vdrain()
            OP(V, lambda v, b=sb32: v.tensor_scalar(
                out=st1[:, b + 8:b + 8 + G4], in0=st1[:, b:b + G4],
                scalar1=1.0 / D, scalar2=None, op0=ALU.mult),
               waits=[("act", a_sq)], inc="dve")
            vdrain()
            OP(V, lambda v, b=sb32: v.tensor_tensor(
                out=st1[:, b + 12:b + 12 + G4], in0=st1[:, b + 8:b + 8 + G4],
                in1=st1[:, b + 8:b + 8 + G4], op=ALU.mult), inc="dve")
            vdrain()
            OP(V, lambda v, b=sb32: v.scalar_tensor_tensor(
                out=st1[:, b + 16:b + 16 + G4], in0=st1[:, b + 4:b + 4 + G4],
                scalar=1.0 / D, in1=st1[:, b + 12:b + 12 + G4],
                op0=ALU.mult, op1=ALU.subtract), inc="dve")
            dv = P.val("dve")
            sdrain()
            a_rt = OP(S, lambda s, b=sb32: s.activation(
                out=st1[:, b + 20:b + 20 + G4], in_=st1[:, b + 16:b + 16 + G4],
                func=AF.Sqrt, bias=eps_b[:]), waits=[("dve", dv)], inc="act")
            OP(V, lambda v, b=sb32: v.reciprocal(
                out=st1[:, b + 20:b + 20 + G4], in_=st1[:, b + 20:b + 20 + G4]),
               waits=[("act", a_rt)], inc="dve")
            vdrain()
            OP(V, lambda v, b=sb32: v.scalar_tensor_tensor(
                out=st1[:, b + 24:b + 24 + G4], in0=st1[:, b + 8:b + 8 + G4],
                scalar=-1.0, in1=st1[:, b + 20:b + 20 + G4],
                op0=ALU.mult, op1=ALU.mult), inc="dve")
            vdrain()
            for k in range(G4):
                i = g_i * G4 + k
                slot = (i % NXR)
                r2 = i % 2
                xsl = xt[:, slot * D:(slot + 1) * D]
                xnfsl = xnf[:, r2 * D:(r2 + 1) * D]
                OP(V, lambda v, xsl=xsl, xnfsl=xnfsl, b=sb32, k=k: v.tensor_scalar(
                    out=xnfsl, in0=xsl, scalar1=st1[:, b + 20 + k:b + 21 + k],
                    scalar2=st1[:, b + 24 + k:b + 25 + k], op0=ALU.mult, op1=ALU.add),
                   waits=[("te", te_tr_done[i - 2] if i >= 2 else 0)], inc="dve")
                vdrain()
                OP(V, lambda v, xnfsl=xnfsl, b=sb32, k=k: v.tensor_reduce(
                    out=st1[:, b + 28 + k:b + 29 + k], in_=xnfsl, axis=AX.X,
                    op=ALU.max, apply_absolute_value=True), inc="dve")
                dvx = P.val("dve")
                for fc in range(HC):
                    if i % 2 == 0:
                        ub = ps_u0 if fc < HC // 2 else ps_u1
                    else:
                        ub = ps_s0[:, 0:TQB] if fc < HC // 2 else ps_s0[:, TQB:2 * TQB]
                    fcl = fc % (HC // 2)
                    OP(TE, lambda t, fc=fc, fcl=fcl, ub=ub, xnfsl=xnfsl: t.matmul(
                        ub[:, fcl * 128:(fcl + 1) * 128],
                        xnfsl[:, fc * 128:(fc + 1) * 128], ident_f32[:],
                        is_transpose=True, start=True, stop=True),
                       waits=[("dve", dvx if fc == 0 else 0),
                              ("dve", dve_ev_done[i - 2] if (fc == 0 and i >= 2) else 0)],
                       inc="te")
                te_tr_done[i] = P.val("te")
                for half in range(2):
                    if i % 2 == 0:
                        ub = ps_u0 if half == 0 else ps_u1
                    else:
                        ub = ps_s0[:, 0:TQB] if half == 0 else ps_s0[:, TQB:2 * TQB]
                    nch = HC // 2
                    OP(V, lambda v, i=i, half=half, ub=ub, nch=nch: v.tensor_copy(
                        out=xn_fm[:].rearrange("p (c t) -> p c t", c=HC)
                            [:, half * nch:(half + 1) * nch, i * 128:(i + 1) * 128],
                        in_=ub[:, 0:nch * 128].rearrange("p (c t) -> p c t", c=nch)),
                       waits=[("te", te_tr_done[i])], inc="dve")
                dve_ev_done[i] = P.val("dve")
                dve_tile_done[i] = P.val("dve")
            vdrain()
            OP(V, lambda v, b=sb32: v.tensor_reduce(
                out=st1[:, b + 28:b + 29], in_=st1[:, b + 28:b + 28 + G4],
                axis=AX.X, op=ALU.max), inc="dve")
            vdrain()
            OP(V, lambda v, b=sb32: v.tensor_tensor(
                out=xmax_acc[:], in0=xmax_acc[:], in1=st1[:, b + 28:b + 29],
                op=ALU.max), inc="dve")
            dve_grp_done[g_i] = P.val("dve")
            act_grp_done[g_i] = P.val("act")
            if getattr(P, "_p1_hook", None) is not None:
                P._p1_hook(g_i)
        dve_p1 = P.val("dve")

        g1 = OP(G, lambda g: g.dma_start(out=cc1_in[:], in_=xmax_acc[:, 0]),
                waits=[("dve", dve_p1), ("gps", gps_const)], inc="gcc")
        c1 = OP(G, lambda g: g.collective_compute(
            "AllReduce", ALU.max, replica_groups=[list(range(8))],
            ins=[cc1_in.ap()], outs=[cc1_out.ap()]), waits=[("gcc", g1)], inc="cc")
        g2 = OP(G, lambda g: g.dma_start(out=gvec[:], in_=cc1_out.ap()[None, :]),
                waits=[("cc", c1)], inc="gcc")

        sbuf_free("xnf", "sq_s", "xt")
        y2 = sbuf("y2", [64, HPC * T], BF16)
        p3_gate = []
        wp_ld = 0
        for hh in range(HPC):
            wp_ld = OP(G, lambda g, hh=hh: g.dma_start(
                out=wproj_b[:, hh * D:(hh + 1) * D],
                in_=wproj[hh * 64:(hh + 1) * 64, :]), waits=p3_gate, inc="swp0")
        bp_ld = OP(G, lambda g: g.dma_start(out=bproj_b[:], in_=bproj[None, :]),
                   waits=p3_gate, inc="sbp")
        dve_wproj = P.val("dve")


        pend_tail = [None]
        nrm_done = {}
        po_store = [0, 0]
        po_evac = [0, 0]
        po_ti = [0]

        def emit_po_tb(tb, dve_gate):
            for co in range(HC):
                r = po_ti[0] % 2
                po_ti[0] += 1
                ub = ps_u0 if r == 0 else ps_u1
                w = [("dve", dve_gate), ("swp0", wp_ld), ("sbp", bp_ld),
                     ("dve", po_evac[r]), ("dve", v_evac[1])]
                for hh in range(HPC):
                    OP(TE, lambda t, hh=hh, co=co, tb=tb, ub=ub: t.matmul(
                        ub[:, 0:TQB],
                        wproj_b[:, hh * D + co * 128: hh * D + (co + 1) * 128],
                        y2[0:64, hh * T + tb * TQB: hh * T + (tb + 1) * TQB],
                        start=(hh == 0), stop=False), waits=w, inc="te")
                    w = []
                OP(TE, lambda t, co=co, ub=ub: t.matmul(
                    ub[:, 0:TQB], bproj_b[0:1, co * 128:(co + 1) * 128],
                    half_row[:], start=False, stop=True), inc="te")
                tpo = P.val("te")
                spn = "spo0" if r == 0 else "spo1"
                ev = OP(V, lambda v, r=r, ub=ub: v.tensor_copy(
                    out=po_st[:, r * TQB:(r + 1) * TQB], in_=ub[:, 0:TQB]),
                    waits=[("te", tpo), (spn, po_store[r])], inc="dve")
                po_evac[r] = ev
                half, tbw = tb // NOB, tb % NOB
                po_store[r] = OP(G, lambda g, half=half, co=co, tbw=tbw, r=r: g.dma_start(
                    out=po_d.ap()[half, co * 128:(co + 1) * 128, tbw * TQB:(tbw + 1) * TQB],
                    in_=po_st[:, r * TQB:(r + 1) * TQB]), waits=[("dve", ev)], inc=spn)

        for pr in range(PAIRS):
            parP = pr % 2
            if pr > 0:
                pair_state = emit_pair_weights(pr, [("te", te_pair_prev)])
                for tb in range(NQB):
                    emit_kq_tb(pair_state, "q", tb, (ps_u0, ps_u1))
                for tb in range(NQB):
                    emit_kq_tb(pair_state, "k", tb, (ps_u0, ps_u1))
                while pair_state["v_i"] < NTK:
                    emit_v_batch(pair_state, (ps_u0, ps_u1), 3)
            dve_negM = emit_norm_tail(pair_state, parP)
            dve_v = P.val("dve")

            hA, hB = 2 * pr, 2 * pr + 1
            for qb in range(NQB):
                span = (qb + 1) * DIAG
                exp_done = {}
                tsc = {}

                def emit_scores(j):
                    par = j % 2
                    ps_sc = ps_s0 if par == 0 else ps_s1
                    w = [("dve", dve_v), ("dve", dve_negM), ("act", act_exp_par[par])]
                    is_diag = j >= qb * DIAG
                    OP(TE, lambda t, j=j, qb=qb, ps_sc=ps_sc, is_diag=is_diag: t.matmul(
                        ps_sc[:, 0:TQB], k_pair[0:64, j * 128:(j + 1) * 128],
                        q_pair[0:64, qb * TQB:(qb + 1) * TQB],
                        start=True, stop=not is_diag, tile_position=(0, 0)),
                       waits=w, inc="te")
                    OP(TE, lambda t, j=j, qb=qb, ps_sc=ps_sc, is_diag=is_diag: t.matmul(
                        ps_sc[:, TQB:2 * TQB], k_pair[64:128, j * 128:(j + 1) * 128],
                        q_pair[64:128, qb * TQB:(qb + 1) * TQB],
                        start=True, stop=not is_diag, tile_position=(64, 0)), inc="te")
                    if is_diag:
                        dlt = j - qb * DIAG
                        OP(TE, lambda t, dlt=dlt, ps_sc=ps_sc: t.matmul(
                            ps_sc[:, 0:TQB], ident_bf[:],
                            masks_bf[:, dlt * TQB:(dlt + 1) * TQB],
                            start=False, stop=True), inc="te")
                        OP(TE, lambda t, dlt=dlt, ps_sc=ps_sc: t.matmul(
                            ps_sc[:, TQB:2 * TQB], ident_bf[:],
                            masks_bf[:, dlt * TQB:(dlt + 1) * TQB],
                            start=False, stop=True), inc="te")
                    tsc[j] = P.val("te")

                def emit_exp(j):
                    par = j % 2
                    ps_sc = ps_s0 if par == 0 else ps_s1
                    exp_done[j] = OP(S, lambda s, par=par, ps_sc=ps_sc, parP=parP: s.activation(
                        out=p_pair[:, par * 2 * TQB:(par + 1) * 2 * TQB],
                        in_=ps_sc[:, 0:2 * TQB], func=AF.Exp, scale=0.125,
                        bias=negM[:, parP:parP + 1]),
                        waits=[("te", tsc[j])], inc="act")
                    act_exp_par[par] = exp_done[j]

                def emit_av(j):
                    par = j % 2
                    avw = [("act", exp_done[j])]
                    if j == 0:
                        avw.append(("dve", dve_y_evac_prev))
                    OP(TE, lambda t, j=j, par=par, first=(j == 0), last=(j == span - 1): t.matmul(
                        ps_avA[0:65, :], v_aug[:, j * 130: j * 130 + 65],
                        p_pair[:, par * 2 * TQB: par * 2 * TQB + TQB],
                        start=first, stop=last), waits=avw, inc="te")
                    OP(TE, lambda t, j=j, par=par, first=(j == 0), last=(j == span - 1): t.matmul(
                        ps_avB[0:65, :], v_aug[:, j * 130 + 65: j * 130 + 130],
                        p_pair[:, par * 2 * TQB + TQB: (par + 1) * 2 * TQB],
                        start=first, stop=last), inc="te")

                emit_scores(0)
                emit_exp(0)
                for j in range(1, span):
                    emit_scores(j)
                    emit_exp(j)
                    if j == 6 and pend_tail[0] is not None:
                        if pr == PAIRS - 1 and qb >= 2:
                            emit_po_tb(qb - 2, nrm_done[qb - 2])
                        pend_tail[0]()
                        pend_tail[0] = None
                    emit_av(j - 1)
                if pend_tail[0] is not None:
                    pend_tail[0]()
                    pend_tail[0] = None
                emit_av(span - 1)
                act_last_exp = exp_done[span - 1]
                tav = P.val("te")
                OP(V, lambda v, hA=hA, qb=qb: v.tensor_copy(
                    out=y2[:, hA * T + qb * TQB: hA * T + (qb + 1) * TQB],
                    in_=ps_avA[0:64, :]), waits=[("te", tav)], inc="dve")
                OP(V, lambda v, hB=hB, qb=qb: v.tensor_copy(
                    out=y2[:, hB * T + qb * TQB: hB * T + (qb + 1) * TQB],
                    in_=ps_avB[0:64, :]), inc="dve")
                OP(V, lambda v: v.tensor_copy(out=denrow[64:65, 0:TQB],
                                              in_=ps_avA[64:65, :]), inc="dve")
                OP(V, lambda v: v.tensor_copy(out=denrow[64:65, TQB:2 * TQB],
                                              in_=ps_avB[64:65, :]), inc="dve")
                dv_den = P.val("dve")
                dve_y_evac_prev = dv_den

                def late_tail(dv_den=dv_den, hA=hA, hB=hB, qb=qb):
                    gs = OP(G, lambda g: g.dma_start(out=den128[:], in_=denrow[64:65, :]),
                            waits=[("dve", dv_den)], inc="gden")
                    OP(V, lambda v: v.reciprocal(out=den128[:], in_=den128[:]),
                       waits=[("gden", gs)], inc="dve")
                    dvr = P.val("dve")
                    gb = OP(G, lambda g: g.dma_start(out=denr[:], in_=den128[:]),
                            waits=[("dve", dvr)], inc="gden")
                    for hh, off in ((hA, 0), (hB, TQB)):
                        tb_ = OP(TE, lambda t, off=off: t.matmul(
                            ps_u1[0:64, 0:TQB], ones64f[:], denr[0:1, off:off + TQB],
                            start=True, stop=True),
                            waits=[("gden", gb), ("dve", v_evac[1]),
                                   ("dve", po_evac[1])], inc="te")
                        nrm = OP(V, lambda v, hh=hh, qb=qb: v.tensor_tensor(
                            out=y2[:, hh * T + qb * TQB: hh * T + (qb + 1) * TQB],
                            in0=y2[:, hh * T + qb * TQB: hh * T + (qb + 1) * TQB],
                            in1=ps_u1[0:64, 0:TQB], op=ALU.mult),
                           waits=[("te", tb_)], inc="dve")
                        v_evac[1] = nrm
                        vdrain()
                    nrm_done[qb] = P.val("dve")
                pend_tail[0] = late_tail
            if pend_tail[0] is not None:
                if pr == PAIRS - 1:
                    emit_po_tb(NQB - 2, nrm_done[NQB - 2])
                pend_tail[0]()
                pend_tail[0] = None
                if pr == PAIRS - 1:
                    emit_po_tb(NQB - 1, P.val("dve"))
            te_pair_prev = P.val("te")
        dve_y = P.val("dve")

        c3 = OP(G, lambda g: g.collective_compute(
            "ReduceScatter", ALU.add,
            replica_groups=[[0, 1], [2, 3], [4, 5], [6, 7]],
            ins=[po_d.ap()], outs=[attn_d.ap()]),
            waits=[("spo0", P.val("spo0")), ("spo1", P.val("spo1"))], inc="cc")

        sbuf_free("y2")
        sbuf_free("po_st", "bproj_b", "wproj_b", "denrow", "p_pair", "sqs",
                  "v_aug", "q_pair", "k_pair", "bvaug", "bq_sb", "waug_b",
                  "wpair_b")
        sbuf_free("xn_fm")
        h_t = sbuf("h_t", [128, IC * TOWN], F16)
        xq_fm = sbuf("xq_fm", [128, HC * TOWN], BF16)
        xt4 = sbuf("xt4", [128, NXR * D], F32)
        sq4 = sbuf("sq4", [128, D], F32)
        xqst = sbuf("xqst", [128, 2 * D], F32)
        q16 = sbuf("q16", [128, 2 * D], F16)
        xq_tm = sbuf("xq_tm", [128, 2 * D], F32)
        OP(V, lambda v: v.tensor_reduce(out=scal[0:1, 4:5], in_=gvec[:], axis=AX.X,
                                        op=ALU.max), waits=[("gcc", g2)], inc="dve")
        vdrain()
        OP(V, lambda v: v.scalar_tensor_tensor(out=scal[0:1, 5:6], in0=scal[0:1, 4:5],
                                               scalar=1.0 / 127.0, in1=scal[0:1, 2:3],
                                               op0=ALU.mult, op1=ALU.mult),
           waits=[("act", act_sig)], inc="dve")
        vdrain()
        OP(V, lambda v: v.reciprocal(out=scal[0:1, 6:7], in_=scal[0:1, 5:6]), inc="dve")
        vdrain()
        OP(V, lambda v: v.tensor_scalar(out=scal[0:1, 7:8], in0=scal[0:1, 5:6],
                                        scalar1=1.0 / 127.0, scalar2=None, op0=ALU.mult),
           inc="dve")
        dve_sx0 = P.val("dve")
        t_bc = OP(TE, lambda t: t.matmul(ps_u1[:, 0:2], ones128f[:],
                                         scal[0:1, 6:8], start=True, stop=True),
                  waits=[("dve", dve_sx0)], inc="te")
        OP(V, lambda v: v.tensor_copy(out=sc_bc[:, 0:2], in_=ps_u1[:, 0:2]),
           waits=[("te", t_bc)], inc="dve")
        dve_scbc = P.val("dve")
        p4_gate = [("dve", P.val("dve")), ("te", P.val("te")),
                   ("act", P.val("act")), ("spo0", P.val("spo0")),
                   ("spo1", P.val("spo1")), ("gden", P.val("gden")),
                   ("gcc", P.val("gcc"))]
        te_xtr = [0] * NT_OWN
        dve_xev = [0] * NT_OWN
        dve_g_done = [0] * NGR_OWN
        act_g_done = [0] * NGR_OWN
        for g_i in range(NGR_OWN):
            gp = g_i % 2
            sb32 = gp * 32
            sxn = "sx0" if gp == 0 else "sx1"
            sds = [0] * G4
            for k in range(G4):
                i = g_i * G4 + k
                slot = i % NXR
                xsl = xt4[:, slot * D:(slot + 1) * D]
                w = list(p4_gate) + [("dve", dve_scbc)]
                if g_i >= 2:
                    w = [("dve", dve_g_done[g_i - 2]), ("act", act_g_done[g_i - 2])]
                sds[k] = OP(SY, lambda s, i=i, xsl=xsl: s.dma_start(
                    out=xsl, in_=x_own[i * 128:(i + 1) * 128, :]), waits=w, inc=sxn)
            for k in range(G4):
                i = g_i * G4 + k
                slot = i % NXR
                xsl = xt4[:, slot * D:(slot + 1) * D]
                sqsl = sq4[:, 0:D]
                OP(V, lambda v, xsl=xsl, c=sb32 + k: v.tensor_reduce(
                    out=st1[:, c:c + 1], in_=xsl, axis=AX.X, op=ALU.add),
                   waits=[(sxn, sds[G4 - 1])], inc="dve")
                sdrain()
                OP(S, lambda s, xsl=xsl, sqsl=sqsl, c=sb32 + 4 + k: s.activation(
                    out=sqsl, in_=xsl, func=AF.Square, accum_out=st1[:, c:c + 1]),
                   waits=[(sxn, sds[G4 - 1]),
                          ("dve", dve_g_done[g_i - 2] if g_i >= 2 else 0)], inc="act")
            a_sq = P.val("act")
            vdrain()
            OP(V, lambda v, b=sb32: v.tensor_scalar(
                out=st1[:, b + 8:b + 8 + G4], in0=st1[:, b:b + G4],
                scalar1=1.0 / D, scalar2=None, op0=ALU.mult),
               waits=[("act", a_sq)], inc="dve")
            vdrain()
            OP(V, lambda v, b=sb32: v.tensor_tensor(
                out=st1[:, b + 12:b + 12 + G4], in0=st1[:, b + 8:b + 8 + G4],
                in1=st1[:, b + 8:b + 8 + G4], op=ALU.mult), inc="dve")
            vdrain()
            OP(V, lambda v, b=sb32: v.scalar_tensor_tensor(
                out=st1[:, b + 16:b + 16 + G4], in0=st1[:, b + 4:b + 4 + G4],
                scalar=1.0 / D, in1=st1[:, b + 12:b + 12 + G4],
                op0=ALU.mult, op1=ALU.subtract), inc="dve")
            dv = P.val("dve")
            sdrain()
            a_rt = OP(S, lambda s, b=sb32: s.activation(
                out=st1[:, b + 20:b + 20 + G4], in_=st1[:, b + 16:b + 16 + G4],
                func=AF.Sqrt, bias=eps_b[:]), waits=[("dve", dv)], inc="act")
            OP(V, lambda v, b=sb32: v.reciprocal(
                out=st1[:, b + 20:b + 20 + G4], in_=st1[:, b + 20:b + 20 + G4]),
               waits=[("act", a_rt)], inc="dve")
            vdrain()
            OP(V, lambda v, b=sb32: v.scalar_tensor_tensor(
                out=st1[:, b + 24:b + 24 + G4], in0=st1[:, b + 8:b + 8 + G4],
                scalar=-1.0, in1=st1[:, b + 20:b + 20 + G4],
                op0=ALU.mult, op1=ALU.mult), inc="dve")
            vdrain()
            OP(V, lambda v, b=sb32: v.tensor_scalar(
                out=st1[:, b + 20:b + 20 + G4], in0=st1[:, b + 20:b + 20 + G4],
                scalar1=sc_bc[:, 0:1], scalar2=None, op0=ALU.mult), inc="dve")
            OP(V, lambda v, b=sb32: v.tensor_scalar(
                out=st1[:, b + 24:b + 24 + G4], in0=st1[:, b + 24:b + 24 + G4],
                scalar1=sc_bc[:, 0:1], scalar2=None, op0=ALU.mult), inc="dve")
            vdrain()
            for k in range(G4):
                i = g_i * G4 + k
                slot = i % NXR
                r2 = i % 2
                xsl = xt4[:, slot * D:(slot + 1) * D]
                sqsl = xqst[:, r2 * D:(r2 + 1) * D]
                f16sl = q16[:, r2 * D:(r2 + 1) * D]
                qsl = xq_tm[:, r2 * D:(r2 + 1) * D]
                OP(V, lambda v, xsl=xsl, sqsl=sqsl, b=sb32, k=k: v.tensor_scalar(
                    out=sqsl, in0=xsl, scalar1=st1[:, b + 20 + k:b + 21 + k],
                    scalar2=st1[:, b + 24 + k:b + 25 + k], op0=ALU.mult, op1=ALU.add),
                   waits=[("act", act_g_done[g_i - 2] if g_i >= 2 else 0),
                          ("act", act_last_exp),
                          ("te", te_xtr[i - 2] if i >= 2 else 0)], inc="dve")
                vdrain()
                OP(V, lambda v, sqsl=sqsl: v.tensor_scalar(
                    out=sqsl, in0=sqsl, scalar1=127.0, scalar2=-128.0,
                    op0=ALU.min, op1=ALU.max), inc="dve")
                vdrain()
                OP(V, lambda v, sqsl=sqsl, f16sl=f16sl: v.tensor_scalar(
                    out=f16sl, in0=sqsl, scalar1=MAGIC, scalar2=None, op0=ALU.add),
                   inc="dve")
                vdrain()
                OP(V, lambda v, f16sl=f16sl, qsl=qsl: v.tensor_scalar(
                    out=qsl, in0=f16sl, scalar1=-MAGIC, scalar2=None, op0=ALU.add),
                   inc="dve")
                dvx = P.val("dve")
                for fc in range(HC):
                    if i % 2 == 0:
                        ub = ps_u0 if fc < HC // 2 else ps_u1
                    else:
                        ub = ps_s0[:, 0:TQB] if fc < HC // 2 else ps_s0[:, TQB:2 * TQB]
                    fcl = fc % (HC // 2)
                    OP(TE, lambda t, fc=fc, fcl=fcl, ub=ub, qsl=qsl: t.matmul(
                        ub[:, fcl * 128:(fcl + 1) * 128],
                        qsl[:, fc * 128:(fc + 1) * 128], ident_f32[:],
                        is_transpose=True, start=True, stop=True),
                       waits=[("dve", dvx if fc == 0 else 0),
                              ("dve", dve_xev[i - 2] if (fc == 0 and i >= 2) else 0)],
                       inc="te")
                te_xtr[i] = P.val("te")
                for half in range(2):
                    if i % 2 == 0:
                        ub = ps_u0 if half == 0 else ps_u1
                    else:
                        ub = ps_s0[:, 0:TQB] if half == 0 else ps_s0[:, TQB:2 * TQB]
                    nch = HC // 2
                    OP(V, lambda v, i=i, half=half, ub=ub, nch=nch: v.tensor_copy(
                        out=xq_fm[:].rearrange("p (c t) -> p c t", c=HC)
                            [:, half * nch:(half + 1) * nch, i * 128:(i + 1) * 128],
                        in_=ub[:, 0:nch * 128].rearrange("p (c t) -> p c t", c=nch)),
                       waits=[("te", te_xtr[i])], inc="dve")
                dve_xev[i] = P.val("dve")
            dve_g_done[g_i] = P.val("dve")
            act_g_done[g_i] = P.val("act")
        dve_xq = P.val("dve")
        te_xq = P.val("te")

        sbuf_free("xq_tm", "q16", "xqst", "sq4", "xt4")
        wgu_s = sbuf("wgu_s", [128, 2 * 2 * D], F32)
        wgu_f16 = sbuf("wgu_f16", [128, 2 * D], F16)
        wgu_int = sbuf("wgu_int", [128, 2 * 2 * D], F32)
        wgT = sbuf("wgT", [128, 2 * HC * 128], BF16)
        wuT = sbuf("wuT", [128, 2 * HC * 128], BF16)
        scg = sbuf("scg", [128, 4], F32)
        sgs = sbuf("sgs", [128, 2 * TQB], F32)
        p4b_gate = [("dve", P.val("dve")), ("te", P.val("te")),
                    ("act", P.val("act"))]

        wload_prev = {0: 0, 1: 0}
        dve_q_done = [0, 0]
        te_gu = [0, 0]
        gu_evac = [0, 0]
        act_sil = [0, 0]
        wln = {"wg0": "sgw0", "wg1": "sgw1", "wu0": "sgu0", "wu1": "sgu1"}
        wtn = {"wg0": "st_w0", "wg1": "st_w1", "wu0": "st_u0", "wu1": "st_u1"}

        def prepA(ic):
            r = ic % 2
            sl_w = slice(r * 2 * D, r * 2 * D + D)
            sl_u = slice(r * 2 * D + D, r * 2 * D + 2 * D)
            res = {}
            for wi, (wdram, sl) in enumerate(((wg_d, sl_w), (wu_d, sl_u))):
                key = ("wg" if wi == 0 else "wu") + str(r)
                w = list(p4b_gate) if ic < 2 else [("dve", dve_q_done[r]),
                                                   ("act", act_q_done[r])]
                sdw = OP(SY, lambda s, ic=ic, wdram=wdram, sl=sl: s.dma_start(
                    out=wgu_s[:, sl], in_=wdram[ic * 128:(ic + 1) * 128, :]),
                    waits=w, inc=wln[key])
                res[wi] = sdw
            for wi, sl in ((0, sl_w), (1, sl_u)):
                key = ("wg" if wi == 0 else "wu") + str(r)
                OP(V, lambda v, sl=sl, wi=wi: v.tensor_reduce(
                    out=misc[:, 2 * wi:2 * wi + 1], in_=wgu_s[:, sl], axis=AX.X,
                    op=ALU.max, apply_absolute_value=True),
                   waits=[(wln[key], res[wi]), ("te", wtr_te[r])], inc="dve")
            vdrain()
            for wi in (0, 1):
                OP(V, lambda v, wi=wi: v.reciprocal(
                    out=misc[:, 2 * wi + 1:2 * wi + 2],
                    in_=misc[:, 2 * wi:2 * wi + 1]), inc="dve")
                OP(V, lambda v, wi=wi, r=r: v.tensor_scalar(
                    out=scg[:, 2 * r + wi:2 * r + wi + 1], in0=misc[:, 2 * wi:2 * wi + 1],
                    scalar1=sc_bc[:, 1:2], scalar2=None, op0=ALU.mult),
                   waits=[("act", act_sil[r]), ("dve", gu_evac[r])], inc="dve")
            vdrain()
            for wi in (0, 1):
                OP(V, lambda v, wi=wi: v.tensor_scalar(
                    out=misc[:, 2 * wi + 1:2 * wi + 2], in0=misc[:, 2 * wi + 1:2 * wi + 2],
                    scalar1=127.0, scalar2=None, op0=ALU.mult), inc="dve")
            dvq = P.val("dve")
            for wi, sl in ((0, sl_w), (1, sl_u)):
                OP(S, lambda s, sl=sl, wi=wi: s.activation(
                    out=wgu_f16[:, wi * D:(wi + 1) * D], in_=wgu_s[:, sl],
                    func=AF.Copy, scale=misc[:, 2 * wi + 1:2 * wi + 2], bias=MAGIC),
                   waits=[("dve", dvq), ("te", wtr_te[r])], inc="act")
            sdrain()
            for wi in (0, 1):
                OP(S, lambda s, wi=wi, r=r: s.activation(
                    out=wgu_int[:, (r * 2 + wi) * D:(r * 2 + wi + 1) * D],
                    in_=wgu_f16[:, wi * D:(wi + 1) * D],
                    func=AF.Copy, scale=1.0, bias=-MAGIC), inc="act")
            act_q_done[r] = P.val("act")
            dve_q_done[r] = P.val("dve")

        def prepB(ic):
            r = ic % 2
            for wi, wT in ((0, wgT), (1, wuT)):
                key = ("wg" if wi == 0 else "wu") + str(r)
                nch = HC // 2
                for half in range(2):
                    ub = ps_u0 if half == 0 else ps_u1
                    for fcl in range(nch):
                        fc = half * nch + fcl
                        OP(TE, lambda t, fc=fc, fcl=fcl, ub=ub, wi=wi, r=r: t.matmul(
                            ub[:, fcl * 128:(fcl + 1) * 128],
                            wgu_int[:, (r * 2 + wi) * D + fc * 128:
                                    (r * 2 + wi) * D + (fc + 1) * 128],
                            ident_f32[:], is_transpose=True, start=True, stop=True),
                           waits=[("act", act_q_done[r]) if (half == 0 and fcl == 0) else
                                  ("act", 0),
                                  ("dve", wtr_evac[half]) if fcl == 0 else
                                  ("dve", 0)],
                           inc="te")
                    tev = P.val("te")
                    wtr_evac[half] = OP(V, lambda v, wT=wT, r=r, half=half,
                                                 ub=ub, nch=nch: v.tensor_copy(
                        out=wT[:, r * HC * 128 + half * nch * 128:
                               r * HC * 128 + (half + 1) * nch * 128],
                        in_=ub[:, 0:nch * 128]),
                        waits=[("te", tev)], inc="dve")
            wload_prev_t[key] = 0
            wtr_te[r] = P.val("te")
            wtr_done[r] = P.val("dve")

        def consume(ic, hook=None):
            r = ic % 2
            dv_scg = dve_q_done[r]
            for tb in range(NOB):
                pt = tb % 2
                pg = ps_s0 if pt == 0 else ps_s1
                w = [("dve", wtr_done[r]),
                     ("dve", dve_xq), ("te", te_xq), ("dve", dv_scg),
                     ("act", act_q_done[r]),
                     ("dve", gu_evac[pt]), ("act", act_sil[pt]),
                     ("act", act_last_exp)]
                for ccx in range(HC):
                    OP(TE, lambda t, ccx=ccx, tb=tb, r=r, pg=pg: t.matmul(
                        pg[:, 0:TQB],
                        wgT[:, r * HC * 128 + ccx * 128: r * HC * 128 + (ccx + 1) * 128],
                        xq_fm[:, ccx * TOWN + tb * TQB: ccx * TOWN + (tb + 1) * TQB],
                        start=(ccx == 0), stop=(ccx == HC - 1)), waits=w, inc="te")
                    w = []
                for ccx in range(HC):
                    OP(TE, lambda t, ccx=ccx, tb=tb, r=r, pg=pg: t.matmul(
                        pg[:, TQB:2 * TQB],
                        wuT[:, r * HC * 128 + ccx * 128: r * HC * 128 + (ccx + 1) * 128],
                        xq_fm[:, ccx * TOWN + tb * TQB: ccx * TOWN + (tb + 1) * TQB],
                        start=(ccx == 0), stop=(ccx == HC - 1)), inc="te")
                te_gu[r] = P.val("te")
                tg = te_gu[r]
                if tb == 0 and hook is not None:
                    hook()
                sil = OP(S, lambda s, pt=pt, pg=pg, r=r: s.activation(
                    out=sgs[:, pt * TQB:(pt + 1) * TQB], in_=pg[:, 0:TQB],
                    func=AF.Silu, scale=scg[:, 2 * r:2 * r + 1]),
                    waits=[("te", tg)], inc="act")
                act_sil[pt] = sil
                OP(V, lambda v, ic=ic, tb=tb, pt=pt, pg=pg, r=r: v.scalar_tensor_tensor(
                    out=h_t[:, ic * TOWN + tb * TQB: ic * TOWN + (tb + 1) * TQB],
                    in0=pg[:, TQB:2 * TQB], scalar=scg[:, 2 * r + 1:2 * r + 2],
                    in1=sgs[:, pt * TQB:(pt + 1) * TQB],
                    op0=ALU.mult, op1=ALU.mult), waits=[("act", sil)], inc="dve")
                gu_evac[pt] = P.val("dve")
            vdrain()
            OP(V, lambda v, ic=ic: v.tensor_reduce(
                out=misc[:, 8:9], in_=h_t[:, ic * TOWN:(ic + 1) * TOWN],
                axis=AX.X, op=ALU.max, apply_absolute_value=True), inc="dve")
            vdrain()
            OP(V, lambda v: v.tensor_tensor(out=hmax_acc[:], in0=hmax_acc[:],
                                            in1=misc[:, 8:9], op=ALU.max), inc="dve")

        wload_prev_t = {"wg0": 0, "wg1": 0, "wu0": 0, "wu1": 0}
        te_gu_tb = [0] * NOB
        wtr_evac = [0, 0]
        wtr_done = [0, 0]
        wtr_te = [0, 0]
        act_q_done = [0, 0]
        prepA(0)
        prepB(0)
        if IC > 1:
            prepA(1)
        for ic in range(IC):
            consume(ic, hook=(lambda ic=ic: prepB(ic + 1)) if ic + 1 < IC else None)
            if ic + 2 < IC:
                prepA(ic + 2)
        dve_h = P.val("dve")

        g3 = OP(G, lambda g: g.dma_start(out=cc2_in[:], in_=hmax_acc[:, 0]),
                waits=[("dve", dve_h)], inc="gcc")
        c2 = OP(G, lambda g: g.collective_compute(
            "AllReduce", ALU.max, replica_groups=[list(range(8))],
            ins=[cc2_in.ap()], outs=[cc2_out.ap()]), waits=[("gcc", g3)], inc="cc")
        g4 = OP(G, lambda g: g.dma_start(out=gvec[:], in_=cc2_out.ap()[None, :]),
                waits=[("cc", c2)], inc="gcc")

        sbuf_free("sgs", "scg", "wuT", "wgT", "wgu_int", "wgu_f16", "wgu_s",
                  "xq_fm")
        wdT = sbuf("wdT", [128, HC * IC * 128], F16)
        sqo_s5 = sbuf("sqo_s5", [128, TOWN], F32)
        hq16 = sbuf("hq16", [128, TOWN], F16)
        xoT_t = sbuf("xoT_t", [128, TOWN // 2], F32)
        attn_t = sbuf("attn_t", [128, TOWN // 2], BF16)
        out_st = sbuf("out_st", [128, TOWN // 2], F32)
        wd_s = sbuf("wd_s", [128, I], F32)
        wd_f16 = sbuf("wd_f16", [128, I], F16)
        te_p4 = P.val("te")
        p5_gate = [("dve", P.val("dve")), ("te", te_p4),
                   ("act", P.val("act")),
                   ("sx0", P.val("sx0")), ("sx1", P.val("sx1")),
                   ("st_w0", P.val("st_w0")), ("st_w1", P.val("st_w1")),
                   ("st_u0", P.val("st_u0")), ("st_u1", P.val("st_u1")),
                   ("sgw0", P.val("sgw0")), ("sgw1", P.val("sgw1")),
                   ("sgu0", P.val("sgu0")), ("sgu1", P.val("sgu1"))]

        WSC = 2.0 ** 14
        TBW = TOWN // 2
        NGQ = IC // 2
        act_hq_tbp = [0, 0]
        hq_state = {"next": 0, "first": True, "p3": 0}

        def hview(ic0, tbp):
            return (h_t[:].rearrange("p (i b q) -> p i b q", i=IC, b=2)
                    [:, ic0:ic0 + 2, tbp, :])

        def emit_hq(n):
            for _ in range(n):
                gi = hq_state["next"]
                if gi >= 2 * NGQ:
                    return
                hq_state["next"] += 1
                tbp, g = divmod(gi, NGQ)
                ic0 = 2 * g
                w1 = [("te", te_p4)] if hq_state["first"] else []
                hq_state["first"] = False
                OP(V, lambda v, ic0=ic0, tbp=tbp: v.tensor_scalar(
                    out=sqo_s5[:].rearrange("p (a q) -> p a q", a=2),
                    in0=hview(ic0, tbp),
                    scalar1=sc_bc[:, 3:4], scalar2=127.0,
                    op0=ALU.mult, op1=ALU.min), waits=w1, inc="dve")
                vdrain()
                w2 = ([("act", hq_state["p3"])] if gi > 0 else [("te", te_p4)])
                OP(V, lambda v: v.tensor_scalar(
                    out=hq16[:, 0:TOWN], in0=sqo_s5[:, 0:TOWN],
                    scalar1=-128.0, scalar2=MAGIC,
                    op0=ALU.max, op1=ALU.add), waits=w2, inc="dve")
                p2v = P.val("dve")
                vdrain()
                OP(S, lambda s, ic0=ic0, tbp=tbp: s.activation(
                    out=hview(ic0, tbp),
                    in_=hq16[:].rearrange("p (a q) -> p a q", a=2),
                    func=AF.Copy, scale=1.0 / WSC, bias=-MAGIC / WSC),
                    waits=[("dve", p2v)], inc="act")
                hq_state["p3"] = P.val("act")
                if g == NGQ - 1:
                    act_hq_tbp[tbp] = P.val("act")

        def emit_cc2_tail():
            OP(V, lambda v: v.tensor_reduce(out=scal[0:1, 16:17], in_=gvec[:], axis=AX.X,
                                            op=ALU.max), waits=[("gcc", g4)], inc="dve")
            vdrain()
            OP(V, lambda v: v.scalar_tensor_tensor(out=scal[0:1, 17:18], in0=scal[0:1, 16:17],
                                                   scalar=1.0 / 127.0, in1=scal[0:1, 3:4],
                                                   op0=ALU.mult, op1=ALU.mult), inc="dve")
            vdrain()
            OP(V, lambda v: v.reciprocal(out=scal[0:1, 18:19], in_=scal[0:1, 17:18]), inc="dve")
            dvsh = P.val("dve")
            tbc = OP(TE, lambda t: t.matmul(ps_u1[:, 0:2], ones128f[:], scal[0:1, 17:19],
                                            start=True, stop=True),
                     waits=[("dve", dvsh)], inc="te")
            OP(V, lambda v: v.tensor_copy(out=sc_bc[:, 2:4], in_=ps_u1[:, 0:2]),
               waits=[("te", tbc)], inc="dve")
            dve_sh = P.val("dve")

        wd_ev_co = [0] * HC
        fin_prev = [0]
        wd_te_prev = [0]
        ev_ping = [0, 0]

        def emit_wd(co, mid_hook=None):
            r = co % 2
            sdn = "swd0" if r == 0 else "swd1"
            w = (list(p5_gate) if co == 0 else
                 [("dve", fin_prev[0]), ("te", wd_te_prev[0])])
            sdw = OP(SY, lambda s, co=co: s.dma_start(
                out=wd_s[:], in_=wd_d[co * 128:(co + 1) * 128, :]), waits=w, inc=sdn)
            OP(V, lambda v: v.tensor_reduce(
                out=misc[:, 4:5], in_=wd_s[:], axis=AX.X,
                op=ALU.max, apply_absolute_value=True),
               waits=[(sdn, sdw)], inc="dve")
            vdrain()
            OP(V, lambda v: v.reciprocal(out=misc[:, 5:6], in_=misc[:, 4:5]), inc="dve")
            OP(V, lambda v: v.tensor_scalar(out=misc[:, 6:7], in0=misc[:, 4:5],
                                            scalar1=1.0 / 127.0, scalar2=None,
                                            op0=ALU.mult), inc="dve")
            vdrain()
            OP(V, lambda v: v.tensor_scalar(out=misc[:, 5:6], in0=misc[:, 5:6],
                                            scalar1=127.0, scalar2=None, op0=ALU.mult),
               inc="dve")
            if mid_hook is not None:
                mid_hook()
                vdrain()
            OP(V, lambda v: v.tensor_scalar(out=misc[:, 7:8], in0=misc[:, 6:7],
                                            scalar1=sc_bc[:, 2:3], scalar2=WSC,
                                            op0=ALU.mult, op1=ALU.mult), inc="dve")
            dvq = P.val("dve")
            vdrain()
            aq = OP(S, lambda s: s.activation(
                out=wd_f16[:], in_=wd_s[:], func=AF.Copy,
                scale=misc[:, 5:6], bias=MAGIC), waits=[("dve", dvq)], inc="act")
            fw = [("act", aq)]
            if co == 0:
                fw.append(("te", te_p4))
            OP(V, lambda v: v.tensor_scalar(
                out=wd_s[:], in0=wd_f16[:],
                scalar1=-MAGIC, scalar2=misc[:, 7:8],
                op0=ALU.add, op1=ALU.mult), waits=fw, inc="dve")
            fin = P.val("dve")
            fin_prev[0] = fin
            vdrain()
            NGW = (IC + 3) // 4
            for g in range(NGW):
                k0 = g * 4
                nk = min(4, IC - k0)
                pb = ps_u0 if g % 2 == 0 else ps_u1
                tw = [("dve", fin), ("act", ev_ping[g % 2])]
                for k in range(nk):
                    OP(TE, lambda t, k=k, k0=k0, pb=pb: t.matmul(
                        pb[:, k * 128:(k + 1) * 128],
                        wd_s[:, (k0 + k) * 128:(k0 + k + 1) * 128],
                        ident_f32[:], is_transpose=True,
                        start=(k == 0), stop=(k == nk - 1)),
                       waits=tw, inc="te")
                    tw = []
                tg = P.val("te")
                OP(S, lambda s, g=g, k0=k0, nk=nk, co=co, pb=pb: s.activation(
                    out=wdT[:, (co * IC + k0) * 128:(co * IC + k0 + nk) * 128],
                    in_=pb[:, 0:nk * 128], func=AF.Copy,
                    scale=1.0, bias=0.0), waits=[("te", tg)], inc="act")
                ev_ping[g % 2] = P.val("act")
            wd_te_prev[0] = P.val("te")
            wd_ev_co[co] = P.val("act")

        NS8 = TBW // 128
        te_xp = [0]
        out_evac = [0, 0]
        ev_prev = [0]
        store_prev = [0]

        tile_seq = [0]

        def emit_tiles(co, tbp):
            if True:
                ti = tile_seq[0]
                tile_seq[0] += 1
                rr = ti % 2
                ub = ps_s0 if rr == 0 else ps_s1
                sds = 0
                for s8 in range(NS8):
                    xw = [("te", te_xp[0])] if ti >= 1 else list(p5_gate)
                    sds = OP("scalar", lambda s, co=co, tbp=tbp, s8=s8: s.dma_start(
                        out=xoT_t[:, s8 * 128:(s8 + 1) * 128],
                        in_=x_own[tbp * TBW + s8 * 128: tbp * TBW + (s8 + 1) * 128,
                                  co * 128:(co + 1) * 128]), waits=xw, inc="sx5a")
                sa = OP(G, lambda g, co=co, tbp=tbp: g.dma_start(
                    out=attn_t[:, 0:TBW],
                    in_=attn_d.ap()[co * 128:(co + 1) * 128,
                                    tbp * TBW:(tbp + 1) * TBW]),
                    waits=[("cc", c3), ("dve", ev_prev[0])] +
                          ([] if ti >= 1 else p5_gate), inc="sat0")
                w = [("act", wd_ev_co[co]), ("sx5a", sds),
                     ("dve", out_evac[rr])]
                NSQ = TQB // 128
                for s8 in range(NS8):
                    OP(TE, lambda t, s8=s8, ub=ub: t.matmul(
                        ub[:, s8 * 128:(s8 + 1) * 128],
                        xoT_t[:, s8 * 128:(s8 + 1) * 128],
                        ident_f32[:], is_transpose=True,
                        start=(s8 % NSQ == 0), stop=False),
                       waits=w, inc="te")
                    w = []
                te_xp[0] = P.val("te")
                NHF = TBW // TQB
                for icx in range(IC):
                    for hf in range(NHF):
                        mw = ([("act", act_hq_tbp[tbp])]
                              if icx == 0 and hf == 0 else [])
                        OP(TE, lambda t, icx=icx, co=co, tbp=tbp, hf=hf, ub=ub:
                           t.matmul(
                            ub[:, hf * TQB:(hf + 1) * TQB],
                            wdT[:, (co * IC + icx) * 128:(co * IC + icx + 1) * 128],
                            h_t[:, icx * TOWN + tbp * TBW + hf * TQB:
                                icx * TOWN + tbp * TBW + (hf + 1) * TQB],
                            start=False, stop=(icx == IC - 1)),
                           waits=mw, inc="te")
                tt = P.val("te")
                ev = OP(V, lambda v, ub=ub: v.tensor_tensor(
                    out=out_st[:, 0:TBW], in0=ub[:, 0:TBW],
                    in1=attn_t[:, 0:TBW], op=ALU.add),
                    waits=[("te", tt), ("sat0", sa), ("so0", store_prev[0])], inc="dve")
                ev_prev[0] = ev
                out_evac[rr] = ev
                store_prev[0] = OP(G, lambda g, co=co, tbp=tbp: g.dma_start(
                    out=out_d[co * 128:(co + 1) * 128, tbp * TBW:(tbp + 1) * TBW],
                    in_=out_st[:, 0:TBW]), waits=[("dve", ev)], inc="so0")

        hpc = max(1, (2 * NGQ + HC - 1) // HC)
        lead = (NGQ + hpc - 1) // hpc
        for co in range(HC):
            emit_wd(co, mid_hook=emit_cc2_tail if co == 0 else None)
            emit_hq(hpc)
            if co >= lead:
                emit_tiles(co - lead, 0)
        emit_hq(2 * NGQ)
        for co in range(HC - lead, HC):
            emit_tiles(co, 0)
        for co in range(HC):
            emit_tiles(co, 1)
        OP(G, lambda g: None, waits=[("so0", P.val("so0"))])

        def runner(eng_name):
            def run(eng):
                for fn, w, inc, amt in P.ops[eng_name]:
                    for s_, val in w:
                        eng.wait_ge(sem_h[s_], val)
                    ins = fn(eng)
                    if inc is not None:
                        assert ins is not None, f"{eng_name} op with inc returned None"
                        ins.then_inc(sem_h[inc], amt)
            return run

        block.sync(runner("sync"))
        block.gpsimd(runner("gpsimd"))
        block.vector(runner("vector"))
        block.scalar(runner("scalar"))
        block.tensor(runner("tensor"))

    return nc, P


def shard_inputs(cfg: Cfg, inp):
    x = np.ascontiguousarray(np.asarray(inp["L0_attn_x"], dtype=np.float32))
    W_attn = np.asarray(inp["L0_attn_W_attn"], dtype=np.float32)
    b_attn = np.asarray(inp["L0_attn_b_attn"], dtype=np.float32)
    W_proj = np.asarray(inp["L0_attn_W_proj"], dtype=np.float32)
    b_proj = np.asarray(inp["L0_attn_b_proj"], dtype=np.float32)
    D, HPC = cfg.D, cfg.HPC
    in_maps = []
    for c in range(8):
        b, h = c // 2, c % 2
        hs = slice(h * HPC * 64, (h + 1) * HPC * 64)
        wq = W_attn[:, 0 * D:1 * D][:, hs]
        wk = W_attn[:, 1 * D:2 * D][:, hs]
        wv = W_attn[:, 2 * D:3 * D][:, hs]
        wqkv = np.ascontiguousarray(np.concatenate([wq, wk, wv], axis=1))
        bq = np.concatenate([b_attn[0 * D:1 * D][hs], b_attn[1 * D:2 * D][hs],
                             b_attn[2 * D:3 * D][hs]])
        in_maps.append({
            "x_full": np.ascontiguousarray(x[b]),
            "x_own": np.ascontiguousarray(x[b, h * cfg.TOWN:(h + 1) * cfg.TOWN]),
            "wqkv": wqkv,
            "bqkv": np.ascontiguousarray(bq),
            "wproj": np.ascontiguousarray(W_proj[hs, :]),
            "bproj": np.ascontiguousarray(b_proj),
            "clip_x": np.asarray(inp["L0_ffn_clip_x"], dtype=np.float32),
            "clip_down": np.asarray(inp["L0_ffn_clip_down"], dtype=np.float32),
            "wg": np.ascontiguousarray(np.asarray(inp["L0_ffn_w_gate"], dtype=np.float32)),
            "wu": np.ascontiguousarray(np.asarray(inp["L0_ffn_w_up"], dtype=np.float32)),
            "wd": np.ascontiguousarray(np.asarray(inp["L0_ffn_w_down"], dtype=np.float32)),
        })
    return in_maps


def assemble(cfg: Cfg, results):
    out = np.empty((cfg.NB, cfg.T, cfg.D), dtype=np.float32)
    for c in range(8):
        b, h = c // 2, c % 2
        out[b, h * cfg.TOWN:(h + 1) * cfg.TOWN, :] = np.asarray(results[c]["out"]).T
    return out




def kernel(**inputs):
    cfg = Cfg()
    in_maps = shard_inputs(cfg, inputs)
    nc, _ = build_program(cfg)
    from concourse.bass_utils import run_bass_kernel_spmd
    res = run_bass_kernel_spmd(nc, in_maps, core_ids=list(range(8)))
    return assemble(cfg, res.results)
